# version 01
# baseline (speedup 1.0000x reference)
import numpy as np
import ml_dtypes
import concourse.bass as bass
import concourse.mybir as mybir
from concourse.bass_utils import run_bass_kernel_spmd
from contextlib import ExitStack

"""Builders for the mk_seq_postln_d2 megakernel (dev module; inlined into kernel.py later)."""
F32 = mybir.dt.float32
BF16 = mybir.dt.bfloat16
AF = mybir.ActivationFunctionType
ALU = mybir.AluOpType
AX = mybir.AxisListType
BF = ml_dtypes.bfloat16
ENGINES = ("tensor", "vector", "scalar", "gpsimd", "sync")


class Prog:

    def __init__(self, auto_drain=False, inc_all=False):
        self.auto_drain = auto_drain
        self.inc_all = inc_all
        self.items = []
        self.ecnt = dict.fromkeys(ENGINES, 0)
        self.dcnt = {}

    def op(self, eng, fn, inc=False, drain=False, waits=()):
        if self.inc_all:
            inc = True
        self.items.append(("op", eng, fn, inc, drain, tuple(waits)))
        if inc:
            self.ecnt[eng] += 1
        return ("e", eng, self.ecnt[eng])

    def dma(self, eng, name, fn, waits=()):
        self.dcnt[name] = self.dcnt.get(name, 0) + 16
        self.items.append(("dma", eng, fn, name, tuple(waits)))
        return ("d", name, self.dcnt[name])

    def now(self, eng):
        return ("e", eng, self.ecnt[eng])

    def dnow(self, name):
        return ("d", name, self.dcnt.get(name, 0))

    def barrier(self, engs):
        snaps = [self.now(e) for e in engs]
        out = []
        for e in engs:
            ws = [s for s in snaps if s[1] != e]
            self.items.append(("waits", e, ws))
        return snaps

    def wait(self, eng, *tokens):
        self.items.append(("waits", eng, tuple(tokens)))

    def build(self, nc, stack, verbose=False):
        esem = {e: stack.enter_context(nc.semaphore(f"es_{e}")) for e in ENGINES}
        dsem = {n: stack.enter_context(nc.semaphore(f"ds_{n}")) for n in self.dcnt}
        block = stack.enter_context(nc.Block())
        per = {e: [] for e in ENGINES}
        for it in self.items:
            per[it[1]].append(it)

        def run_engine(e, eng_items):
            seen = {}

            def handle(engine):
                cnt = 0
                dc = {}
                for it in eng_items:
                    if it[0] == "waits":
                        for tk in it[2]:
                            _wait(engine, tk)
                        continue
                    if it[0] == "op":
                        _, _, fn, inc, drain, waits = it
                        for tk in waits:
                            _wait(engine, tk)
                        instr = fn(engine)
                        if drain or (self.auto_drain and e in ("vector", "scalar", "gpsimd")):
                            instr = engine.drain()
                        if inc:
                            cnt += 1
                            (instr if instr is not None else engine.nop()).then_inc(esem[e], 1)
                    else:
                        _, _, fn, name, waits = it
                        for tk in waits:
                            _wait(engine, tk)
                        d = fn(engine)
                        d.then_inc(dsem[name], 16)

            def _wait(engine, tk):
                kind, key, val = tk
                if val <= 0:
                    return
                if kind == "e" and key == e:
                    return
                prev = seen.get((kind, key), -1)
                if val <= prev:
                    return
                seen[(kind, key)] = val
                sem = esem[key] if kind == "e" else dsem[key]
                engine.wait_ge(sem, val)

            return handle

        for e in ENGINES:
            getattr(block, e if e != "sync" else "sync")(run_engine(e, per[e]))
        return esem, dsem


def ceil_div(a, b):
    return (a + b - 1) // b


def build_mamba_fwd(nc, p, stack, cfg, tens):
    L, DM, DI, NS, DTR, Tc = cfg["L"], cfg["DM"], cfg["DI"], cfg["NS"], cfg["DTR"], cfg["Tc"]
    NCH = L // Tc
    DTt = DI // 128
    KT = DM // 128
    MT = Tc // 128
    NP = DTR + 2 * NS

    sb = lambda n, s, d=F32: stack.enter_context(nc.sbuf_tensor(n, s, d))
    dram = lambda n, s, d=BF16: nc.dram_tensor(n, s, d, kind="Internal").ap()

    xbf = dram("xbf", [L, DM])
    inp_bf = dram("inp_bf", [2 * DI, DM])
    outp_bf = dram("outp_bf", [DM, DI])
    xp_bf = dram("xp_bf", [128, DI])
    dtw_bf = dram("dtw_bf", [DI, 128])
    outT_dram = dram("outT_dram", [DI, DM])

    in_projT = sb("in_projT", [128, KT, 2 * DI], BF16)
    x_projT = sb("x_projT", [128, DTt, 128], BF16)
    dt_wT = sb("dt_wT", [128, DI], BF16)
    sel_sb = sb("sel_sb", [NS, NS * 128], BF16)
    A_sb = sb("A_sb", [128, DTt, NS])
    Alog_sb = A_sb
    dtb_sb = sb("dtb_sb", [128, DTt])
    convb_sb = sb("convb_sb", [128, DTt])
    D_sb = sb("D_sb", [128, DTt])
    convw_sb = sb("convw_sb", [128, DTt, 4])
    x0cT = sb("x0cT", [128, KT, Tc], BF16)
    conv_in = sb("conv_in", [128, 4, Tc + 3], BF16)
    halo = sb("halo", [128, DTt, 3])
    x_silu = sb("x_silu", [128, DTt, Tc], BF16)
    delta = sb("delta", [128, DTt, Tc], BF16)
    u_bf = sb("u_bf", [128, DTt, Tc], BF16)
    y_acc = sb("y_acc", [128, DTt, Tc])
    carry = sb("carry", [128, DTt * NS])
    dtr_bf = sb("dtr_bf", [DTR, Tc], BF16)
    Btile = sb("Btile", [NS, Tc], BF16)
    Ctile = sb("Ctile", [NS, Tc], BF16)
    dA_b = sb("dA_b", [128, 4, Tc])
    dBu_b = sb("dBu_b", [128, 4, Tc], BF16)
    h_b = sb("h_b", [128, 4, Tc], BF16)
    m_b = sb("m_b", [128, 4, Tc], BF16)
    z_bf = m_b
    tmp_c = dBu_b
    yq = delta
    outT_t = sb("outT_t", [128, 2, DM], BF16)
    attn_sb = sb("attn_sb", [128, DM])
    sp_t = dA_b
    sp_r = sb("sp_r", [128, 4, Tc], BF16)
    stage_t = outT_t[:, 0, :]
    psum = stack.enter_context(nc.psum_tensor("psum", [128, 4096], F32))
    bank = lambda b, par=128: psum[0:par, b * 512:(b + 1) * 512]

    T, V, S, G, Y = "tensor", "vector", "scalar", "gpsimd", "sync"

    c1 = p.dma(G, "cast", lambda e: e.dma_start(out=xbf, in_=tens["x0"]))
    c2 = p.dma(G, "cast", lambda e: e.dma_start(out=inp_bf, in_=tens["in_proj"]))
    c3 = p.dma(G, "cast", lambda e: e.dma_start(out=outp_bf, in_=tens["out_proj"]))
    c4 = p.dma(G, "cast", lambda e: e.dma_start(out=xp_bf[0:NP, :], in_=tens["x_proj"]))
    c5 = p.dma(G, "cast", lambda e: e.dma_start(out=dtw_bf[:, 0:DTR], in_=tens["dt_w"]))
    cast_done = p.dnow("cast")
    for nm, dst in [("A_log", Alog_sb), ("dt_b", dtb_sb), ("conv_b", convb_sb),
                    ("D", D_sb), ("conv_w", convw_sb), ("sel16", sel_sb)]:
        src = tens[nm]
        p.dma(Y, "ld", lambda e, dst=dst, src=src: e.dma_start(out=dst[:], in_=src))
    ld_done = p.dnow("ld")
    for k in range(KT):
        p.dma(Y, "tp", lambda e, k=k: e.dma_start(
            out=in_projT[:, k, :], in_=inp_bf[:, k * 128:(k + 1) * 128], transpose=True),
            waits=(cast_done,))
    for dt in range(DTt):
        p.dma(Y, "tp", lambda e, dt=dt: e.dma_start(
            out=x_projT[:, dt, :], in_=xp_bf[:, dt * 128:(dt + 1) * 128], transpose=True),
            waits=(cast_done,))
    p.dma(Y, "tp", lambda e: e.dma_start(out=dt_wT[:], in_=dtw_bf, transpose=True),
          waits=(cast_done,))
    for kt in range(DTt):
        tpt = p.dma(Y, "tp", lambda e, kt=kt: e.dma_start(
            out=stage_t[:], in_=outp_bf[:, kt * 128:(kt + 1) * 128], transpose=True),
            waits=(cast_done, p.dnow("stg")))
        p.dma(G, "stg", lambda e, kt=kt: e.dma_start(
            out=outT_dram[kt * 128:(kt + 1) * 128, :], in_=stage_t[:]), waits=(tpt,))
    e1 = p.op(S, lambda e: e.activation(A_sb[:], Alog_sb[:], AF.Exp), inc=True,
              waits=(ld_done,))
    p.op(V, lambda e: e.tensor_scalar(A_sb[:], A_sb[:], -1.0, None, ALU.mult), inc=True,
         waits=(e1,), drain=True)
    p.op(V, lambda e: e.memset(halo[:], 0.0), inc=True)
    p.op(V, lambda e: e.memset(carry[:], 0.0), inc=True)
    prep_T = p.now(T); prep_V = p.now(V); prep_S = p.now(S)
    tp_done = p.dnow("tp")
    p.wait(T, tp_done, p.now(V), p.now(S))
    p.wait(V, tp_done)
    p.wait(S, tp_done)

    XB = (0, 1, 5, 6)
    for ch in range(NCH):
        t0 = ch * Tc
        xw = []
        for k in range(KT):
            xw.append(p.dma(Y, "x0c", lambda e, k=k, t0=t0: e.dma_start(
                out=x0cT[:, k, :], in_=xbf[t0:t0 + Tc, k * 128:(k + 1) * 128],
                transpose=True), waits=(p.now(V),)))
        x0c_ok = xw[-1]

        silu_snap = {}
        for g in range(0, DTt, 4):
            quad = (g, g + 1, g + 2, g + 3)
            mm_snap = {}
            for dt in quad:
                def mm_xz(e, dt=dt):
                    ins = None
                    for k in range(KT):
                        ins = e.matmul(bank(XB[dt % 4]),
                                       in_projT[:, k, dt * 128:(dt + 1) * 128],
                                       x0cT[:, k, :], start=(k == 0), stop=(k == KT - 1))
                    return ins
                mm_snap[dt] = p.op(T, mm_xz, inc=True,
                                   waits=(x0c_ok, silu_snap.get(g - 4, ("e", S, 0))))
            ev_snap = {}
            for dt in quad:
                ev_snap[dt] = p.op(S, lambda e, dt=dt: e.copy(
                    out=conv_in[:, dt % 4, 3:], in_=bank(XB[dt % 4])),
                    inc=True, waits=(mm_snap[dt],))
            for dt in quad:
                p.op(V, lambda e, dt=dt: e.tensor_copy(conv_in[:, dt % 4, 0:3], halo[:, dt, :]),
                     waits=(ev_snap[dt], silu_snap.get(g - 4, ("e", S, 0))))
            for j in range(4):
                for dt in quad:
                    if j == 0:
                        p.op(V, lambda e, dt=dt: e.tensor_scalar(
                            sp_t[:, dt % 4, :], conv_in[:, dt % 4, 0:Tc],
                            convw_sb[:, dt, 0:1], None, ALU.mult))
                    else:
                        p.op(V, lambda e, dt=dt, j=j: e.scalar_tensor_tensor(
                            sp_t[:, dt % 4, :], conv_in[:, dt % 4, j:Tc + j],
                            convw_sb[:, dt, j:j + 1], sp_t[:, dt % 4, :], ALU.mult, ALU.add))
            conv_v = {}
            for dt in quad:
                conv_v[dt] = p.op(V, lambda e, dt=dt: e.tensor_copy(
                    halo[:, dt, :], conv_in[:, dt % 4, Tc:Tc + 3]), inc=True)
            for dt in quad:
                silu_snap[g] = p.op(S, lambda e, dt=dt: e.activation(
                    x_silu[:, dt, :], sp_t[:, dt % 4, :], AF.Silu, bias=convb_sb[:, dt:dt + 1]),
                    inc=True, waits=(conv_v[dt],))
            for dt in quad:
                def mm_xdbl(e, dt=dt):
                    st = (dt == 0)
                    sp = (dt == DTt - 1)
                    e.matmul(bank(2, DTR), x_projT[:, dt, 0:DTR], x_silu[:, dt, :],
                             start=st, stop=sp)
                    e.matmul(bank(3, NS), x_projT[:, dt, DTR:DTR + NS], x_silu[:, dt, :],
                             start=st, stop=sp)
                    return e.matmul(bank(4, NS), x_projT[:, dt, DTR + NS:DTR + 2 * NS],
                                    x_silu[:, dt, :], start=st, stop=sp)
                p.op(T, mm_xdbl, inc=True, waits=(silu_snap[g],))
        xdbl_T = p.now(T)
        ev_dtr = p.op(S, lambda e: e.copy(out=dtr_bf[:], in_=bank(2, DTR)), inc=True,
                      waits=(xdbl_T,))
        ev_B = p.op(S, lambda e: e.copy(out=Btile[:], in_=bank(3, NS)), inc=True)
        ev_C = p.op(S, lambda e: e.copy(out=Ctile[:], in_=bank(4, NS)), inc=True)

        dmm = {}
        dv_hist = {}
        for g in range(0, DTt, 4):
            quad = (g, g + 1, g + 2, g + 3)
            for dt in quad:
                dmm[dt] = p.op(T, lambda e, dt=dt: e.matmul(
                    bank(XB[dt % 4]), dt_wT[0:DTR, dt * 128:(dt + 1) * 128], dtr_bf[:],
                    start=True, stop=True), inc=True,
                    waits=(ev_dtr, p.now(S)))
            sp_done = {}
            for dt in quad:
                p.op(S, lambda e, dt=dt: e.activation(
                    sp_t[:, dt % 4, 0:Tc], bank(XB[dt % 4]), AF.Abs, bias=dtb_sb[:, dt:dt + 1]),
                    waits=(dmm[dt], dv_hist.get(g - 4, ("e", V, 0))))
            for dt in quad:
                p.op(S, lambda e, dt=dt: e.activation(
                    sp_r[:, dt % 4, 0:Tc], bank(XB[dt % 4]),
                    AF.Relu, bias=dtb_sb[:, dt:dt + 1]))
            for dt in quad:
                p.op(S, lambda e, dt=dt: e.activation(
                    sp_t[:, dt % 4, 0:Tc], sp_t[:, dt % 4, 0:Tc], AF.Exp, scale=-1.0))
            for dt in quad:
                sp_done[dt] = p.op(S, lambda e, dt=dt: e.activation(
                    sp_t[:, dt % 4, 0:Tc], sp_t[:, dt % 4, 0:Tc], AF.Ln, bias=1.0), inc=True)
            dv = {}
            for dt in quad:
                dv[dt] = p.op(V, lambda e, dt=dt: e.tensor_tensor(
                    delta[:, dt, :], sp_r[:, dt % 4, 0:Tc],
                    sp_t[:, dt % 4, 0:Tc], ALU.add),
                    inc=True, waits=(sp_done[dt],))
            for dt in quad:
                p.op(V, lambda e, dt=dt: e.tensor_tensor(
                    u_bf[:, dt, :], delta[:, dt, :], x_silu[:, dt, :], ALU.mult), inc=True)
            dv_hist[g] = p.now(V)
        yz = p.op(V, lambda e: e.memset(y_acc[:], 0.0), inc=True)
        p.barrier((T, V, S))

        scan_v_snap = {}
        n_last_v = {}
        for n in range(NS):
            bB = 2 if n % 2 == 0 else 3
            bC = 7 if n % 2 == 0 else 4
            war = n_last_v.get(n - 2, ("e", V, 0))
            selL = sel_sb[:, n * 128:(n + 1) * 128]
            bc1 = p.op(T, lambda e, bB=bB, selL=selL: e.matmul(
                bank(bB), selL, Btile[:], start=True, stop=True), waits=(ev_B, ev_C, war))
            bc2 = p.op(T, lambda e, bC=bC, selL=selL: e.matmul(
                bank(bC), selL, Ctile[:], start=True, stop=True), inc=True)
            for g in range(0, DTt, 4):
                quad = (g, g + 1, g + 2, g + 3)
                da = {}
                for dt in quad:
                    prev = (scan_v_snap.get((n, dt - 4))
                            or scan_v_snap.get((n - 1, DTt - 4 + (dt % 4)))
                            or ("e", V, 0))
                    da[dt] = p.op(S, lambda e, dt=dt, n=n: e.activation(
                        dA_b[:, dt % 4, :], delta[:, dt, :], AF.Exp,
                        scale=A_sb[:, dt, n:n + 1]), inc=True, waits=(prev,))
                for dt in quad:
                    p.op(V, lambda e, dt=dt, bB=bB: e.tensor_tensor(
                        dBu_b[:, dt % 4, :], u_bf[:, dt, :], bank(bB), ALU.mult),
                        waits=(da[dt], bc2))
                for dt in quad:
                    scan_v_snap[(n, dt)] = p.op(V, lambda e, dt=dt, n=n: e.tensor_tensor_scan(
                        h_b[:, dt % 4, :], dA_b[:, dt % 4, :], dBu_b[:, dt % 4, :],
                        carry[:, dt * NS + n:dt * NS + n + 1], ALU.mult, ALU.add), inc=True)
                for dt in quad:
                    p.op(V, lambda e, dt=dt, bC=bC: e.tensor_tensor(
                        m_b[:, dt % 4, :], h_b[:, dt % 4, :], bank(bC), ALU.mult))
                for dt in quad:
                    p.op(V, lambda e, dt=dt, n=n: e.tensor_copy(
                        carry[:, dt * NS + n:dt * NS + n + 1], h_b[:, dt % 4, Tc - 1:Tc]))
                for dt in quad:
                    p.op(V, lambda e, dt=dt: e.tensor_tensor(
                        y_acc[:, dt, :], y_acc[:, dt, :], m_b[:, dt % 4, :], ALU.add),
                        inc=(dt == DTt - 1))
            n_last_v[n] = p.now(V)
        p.barrier((T, V, S))

        for g in range(0, DTt, 4):
            quad = (g, g + 1, g + 2, g + 3)
            zmm = {}
            for dt in quad:
                def mm_z(e, dt=dt):
                    ins = None
                    for k in range(KT):
                        ins = e.matmul(bank(XB[dt % 4]),
                                       in_projT[:, k, (DI + dt * 128):(DI + (dt + 1) * 128)],
                                       x0cT[:, k, :], start=(k == 0), stop=(k == KT - 1))
                    return ins
                zmm[dt] = p.op(T, mm_z, inc=True, waits=(p.now(V), p.now(S)))
            zs = {}
            for dt in quad:
                zs[dt] = p.op(S, lambda e, dt=dt: e.activation(
                    z_bf[:, dt % 4, :], bank(XB[dt % 4]), AF.Silu), inc=True, waits=(zmm[dt],))
            for dt in quad:
                p.op(V, lambda e, dt=dt: e.tensor_scalar(
                    tmp_c[:, dt % 4, :], x_silu[:, dt, :], D_sb[:, dt:dt + 1], None, ALU.mult),
                    waits=(zs[dt],))
            for dt in quad:
                p.op(V, lambda e, dt=dt: e.tensor_tensor(
                    tmp_c[:, dt % 4, :], y_acc[:, dt, :], tmp_c[:, dt % 4, :], ALU.add))
            for dt in quad:
                p.op(V, lambda e, dt=dt: e.tensor_tensor(
                    yq[:, dt, :], tmp_c[:, dt % 4, :], z_bf[:, dt % 4, :], ALU.mult), inc=True)
        p.barrier((T, V, S))

        dmm2 = {}
        for dt in range(DTt):
            lw = (dmm2[dt - 2],) if dt - 2 in dmm2 else (p.dnow("stg"),)
            otw = p.dma(Y, f"outT{dt % 2}", lambda e, dt=dt: e.dma_start(
                out=outT_t[:, dt % 2, :], in_=outT_dram[dt * 128:(dt + 1) * 128, :]),
                waits=lw)

            def mm_out(e, dt=dt):
                st = (dt == 0)
                sp = (dt == DTt - 1)
                ins = None
                for m in range(MT):
                    for hh in range(2):
                        ins = e.matmul(bank(m * 2 + hh), yq[:, dt, m * 128:(m + 1) * 128],
                                       outT_t[:, dt % 2, hh * 512:(hh + 1) * 512],
                                       start=st, stop=sp)
                return ins
            dmm2[dt] = p.op(T, mm_out, inc=True, waits=(otw,))
        out_T = p.now(T)
        for m in range(MT):
            evm = p.op(S, lambda e, m=m: e.copy(out=attn_sb[:], in_=psum[:, m * 1024:(m + 1) * 1024]),
                       inc=True, waits=(out_T, p.dnow("st")))
            p.dma(G, "st", lambda e, m=m, t0=t0: e.dma_start(
                out=tens["out"][t0 + m * 128: t0 + (m + 1) * 128, :], in_=attn_sb[:]),
                waits=(evm,))
        p.barrier((T, V, S, G, Y))
    p.wait(G, p.dnow("st"))
    p.op(G, lambda e: e.nop())
    return



"""Phase-B builders: x1=LN(x0+pf+pr); L0_ffn; L1_attn (GLA block); L1_ffn; final LN.

Token layout per core: 2048 tokens = 4 batches x 512 seq positions, b-major
(row = b*512 + l_local). 16 token-tiles of 128. GLA "time" = batch axis (4
steps); sub s (s=0..3) covers tiles {s, 4+s, 8+s, 12+s} = same l-range, b=0..3.
Canonical activation layout: (token-part 128, feature-free) f32 residual.
Matmul inputs are transposed on demand via TensorE into xT (feature-part,
token-free) bf16.

PSUM rule: one accumulation group per bank (first_mm clears the whole bank).
Same-engine dependent V/S ops carry drain=True (v1; modest op counts here).
"""
F32 = mybir.dt.float32
BF16 = mybir.dt.bfloat16
AF = mybir.ActivationFunctionType
ALU = mybir.AluOpType
AX = mybir.AxisListType
BF = ml_dtypes.bfloat16

T, V, S, G, Y = "tensor", "vector", "scalar", "gpsimd", "sync"

D = 1024
TT = 16
KT = 8
N = 2048
NH = 16
DK = 64
DHID = 4096
E0, R0 = 6, 16
E1, F1, FS = 8, 512, 1024
GLA_TEMP = 16.0
DBG_SUB = 0


class Ctx:
    pass


def alloc_common(nc, p, stack, tens):
    c = Ctx()
    c.nc, c.p, c.stack, c.tens = nc, p, stack, tens
    sb = lambda n, s, d=F32: stack.enter_context(nc.sbuf_tensor(n, s, d))
    dram = lambda n, s, d=BF16: nc.dram_tensor(n, s, d, kind="Internal").ap()
    c.dram = dram

    c.resid = sb("resid", [128, TT, D])
    c.dacc = sb("dacc", [128, TT, D], BF16)
    c.xT = sb("xT", [128, KT, N], BF16)
    c.dump = sb("dump", [128, 4, D])
    c.cols = sb("cols", [128, 64])
    c.qkv = sb("qkv", [128, 12, 512], BF16)
    c.qb = c.qkv[:, 0:4, :]
    c.kb = c.qkv[:, 4:8, :]
    c.vb = c.qkv[:, 8:12, :]
    c.ogT = c.qkv.rearrange("p a b -> p (a b)")[:, 0:KT * 512].rearrange(
        "p (a b) -> p a b", b=512)
    c.og = sb("og", [128, 4, D], BF16)
    c.Pb = sb("Pb", [128, 4, 512], BF16)
    c.Pib = sb("Pib", [128, 4, 512], BF16)
    c.oac = None
    c.sc = sb("sc", [128, 12, 8])
    c.u1 = sb("u1", [128, 4, 512], BF16)
    c.gh_f = c.u1[:, 0:2, :]
    c.uh_f = c.u1[:, 2:4, :]
    c.ghT = c.u1[0:16].rearrange("p a b -> p (a b)")
    c.hT = sb("hT", [128, 512], BF16)
    c.hTs = sb("hTs", [16, 128], BF16)
    c.hTr = sb("hTr", [96, 128], BF16)
    c.w8 = sb("w8", [128, TT, 8])
    c.wbufA = sb("wbufA", [128, 2, 1024], BF16)
    c.wg1T = sb("wg1T", [128, KT, 16], BF16)
    c.wg2T = sb("wg2T", [128, D], BF16)
    c.sharedAT = sb("sharedAT", [128, KT, 16], BF16)
    c.sharedBT = sb("sharedBT", [128, D], BF16)
    c.routedAT = sb("routedAT", [128, KT, E0 * R0], BF16)
    c.routedBcat = sb("routedBcat", [96, D], BF16)
    c.routerWT = sb("routerWT", [128, KT, 16], BF16)
    c.router1 = sb("router1", [128, KT, 8], BF16)
    c.bg1c = sb("bg1c", [16, 1])
    c.rbc = sb("rbc", [128, 8])
    c.bcast = {}
    for nm in ("base_b", "bg2", "bog", "scale1", "scale2"):
        c.bcast[nm] = sb(f"bc_{nm}", [128, D], BF16)
    c.ones1 = sb("ones1", [1, 128])
    c.rowbuf = None
    c.identF = sb("identF_sb", [128, 128])
    c.epsc = sb("epsc", [128, 1])
    c.identB = sb("identB_sb", [128, 128], BF16)
    c.psum = stack.enter_context(nc.psum_tensor("psumB", [128, 4096], F32))
    c.bank = lambda b, par=128: c.psum[0:par, b * 512:(b + 1) * 512]
    _psB = c.psum.bitcast(BF16)
    c.bankB = lambda b, par=128: _psB[0:par, b * 1024:(b + 1) * 1024]
    c.nstat = 0
    c.rowbuf = c.dump[0:1, 0, :]
    c.oac = c.dump[:, 0:2, :].rearrange("p a (b f) -> p (a b) f", f=512)

    p.dma(Y, "wld", lambda e: e.dma_start(out=c.identF[:], in_=tens["identF"]))
    p.dma(Y, "wld", lambda e: e.dma_start(out=c.identB[:], in_=tens["identB"]))

    c.wdram = {}

    def cast(nm, src, shape):
        dd = dram(f"cst_{nm}", shape)
        p.dma(G, "cast", lambda e, dd=dd, src=src: e.dma_start(out=dd, in_=src))
        c.wdram[nm] = dd
        return dd

    c.trans_specs = []

    def prep_transposed(nm, src, rows, cols):
        raw = cast(nm + "_raw", src, [rows, cols])
        out = dram(nm + "T", [cols, rows])
        c.trans_specs.append((nm, raw, out, rows, cols))
        return out

    tn = tens
    c.baseWT = prep_transposed("baseW", tn["L0_ffn_base_W"], D, D)
    c.wqT = prep_transposed("wq", tn["L1_attn_wq"], D, D)
    c.wkT = prep_transposed("wk", tn["L1_attn_wk"], D, D)
    c.wvT = prep_transposed("wv", tn["L1_attn_wv"], D, D)
    c.wogT = prep_transposed("wog", tn["L1_attn_wog"], D, D)
    c.woT = prep_transposed("wo", tn["L1_attn_wo"], D, D)
    c.wgateT = prep_transposed("wgate", tn["L1_attn_w_gate"], DHID, D)
    c.wupT = prep_transposed("wup", tn["L1_attn_w_up"], DHID, D)
    c.wdownT = prep_transposed("wdown", tn["L1_attn_w_down"], D, DHID)
    c.f1gate = cast("f1gate", tn["L1_ffn_w_gate"], [E1, D, F1])
    c.f1up = cast("f1up", tn["L1_ffn_w_up"], [E1, D, F1])
    c.f1down = cast("f1down", tn["L1_ffn_w_down"], [E1, F1, D])
    c.wsgate = cast("wsgate", tn["L1_ffn_ws_gate"], [D, FS])
    c.wsup = cast("wsup", tn["L1_ffn_ws_up"], [D, FS])
    c.wsdown = cast("wsdown", tn["L1_ffn_ws_down"], [FS, D])
    c.sA_raw = cast("sA_raw", tn["L0_ffn_shared_A"], [R0, D])
    c.wg1_raw = cast("wg1_raw", tn["L1_attn_wg1"], [16, D])
    c.rW_raw = dram("rW_raw", [16, D])
    p.dma(G, "cast", lambda e: e.dma_start(out=c.rW_raw[0:E0, :],
                                           in_=tn["L0_ffn_router_W"]))
    c.wg2_pad = dram("wg2_pad", [D, 128])
    p.dma(G, "cast", lambda e: e.dma_start(out=c.wg2_pad[:, 0:16],
                                           in_=tn["L1_attn_wg2"]))
    c.sB_pad = dram("sB_pad", [D, 128])
    p.dma(G, "cast", lambda e: e.dma_start(out=c.sB_pad[:, 0:R0],
                                           in_=tn["L0_ffn_shared_B"]))
    c.rA_raw = []
    c.rB_pad = []
    for ei in range(E0):
        ra = dram(f"rA_raw{ei}", [R0, D])
        p.dma(G, "cast", lambda e, ra=ra, ei=ei: e.dma_start(
            out=ra, in_=tn["L0_ffn_routed_A"][ei]))
        c.rA_raw.append(ra)
        rb = dram(f"rB_pad{ei}", [D, 128])
        p.dma(G, "cast", lambda e, rb=rb, ei=ei: e.dma_start(
            out=rb[:, 0:R0], in_=tn["L0_ffn_routed_B"][ei]))
        c.rB_pad.append(rb)
    c.r1_raw = cast("r1_raw", tn["L1_ffn_router_w"], [D, 8])
    cast_done = p.dnow("cast")

    stg = c.wbufA[:, 0, :]
    for nm, raw, out, rows, cols in c.trans_specs:
        for kk in range(cols // 128):
            for r0 in range(0, rows, 1024):
                rc = min(1024, rows - r0)
                tpt = p.dma(Y, "tps", lambda e, raw=raw, kk=kk, r0=r0, rc=rc: e.dma_start(
                    out=stg[:, 0:rc], in_=raw[r0:r0 + rc, kk * 128:(kk + 1) * 128],
                    transpose=True), waits=(cast_done, p.dnow("stg")))
                p.dma(G, "stg", lambda e, out=out, kk=kk, r0=r0, rc=rc: e.dma_start(
                    out=out[kk * 128:(kk + 1) * 128, r0:r0 + rc], in_=stg[:, 0:rc]),
                    waits=(tpt,))
    for kk in range(KT):
        p.dma(Y, "tp", lambda e, kk=kk: e.dma_start(
            out=c.sharedAT[:, kk, :], in_=c.sA_raw[:, kk * 128:(kk + 1) * 128],
            transpose=True), waits=(cast_done,))
        p.dma(Y, "tp", lambda e, kk=kk: e.dma_start(
            out=c.wg1T[:, kk, :], in_=c.wg1_raw[:, kk * 128:(kk + 1) * 128],
            transpose=True), waits=(cast_done,))
        p.dma(Y, "tp", lambda e, kk=kk: e.dma_start(
            out=c.routerWT[:, kk, :], in_=c.rW_raw[:, kk * 128:(kk + 1) * 128],
            transpose=True), waits=(cast_done,))
        for ei in range(E0):
            p.dma(Y, "tp", lambda e, kk=kk, ei=ei: e.dma_start(
                out=c.routedAT[:, kk, ei * R0:(ei + 1) * R0],
                in_=c.rA_raw[ei][:, kk * 128:(kk + 1) * 128], transpose=True),
                waits=(cast_done,))
        p.dma(Y, "tp", lambda e, kk=kk: e.dma_start(
            out=c.wg2T[:, kk * 128:(kk + 1) * 128],
            in_=c.wg2_pad[kk * 128:(kk + 1) * 128, :], transpose=True),
            waits=(cast_done,))
        p.dma(Y, "tp", lambda e, kk=kk: e.dma_start(
            out=c.sharedBT[:, kk * 128:(kk + 1) * 128],
            in_=c.sB_pad[kk * 128:(kk + 1) * 128, :], transpose=True),
            waits=(cast_done,))
        p.dma(Y, "wld", lambda e, kk=kk: e.dma_start(
            out=c.router1[:, kk, :], in_=c.r1_raw[kk * 128:(kk + 1) * 128, :]),
            waits=(cast_done,))
    rb_stage = c.dacc[:, 0, :]
    for ei in range(E0):
        for kk in range(KT):
            tpt = p.dma(Y, "tpr", lambda e, ei=ei, kk=kk: e.dma_start(
                out=rb_stage[:, 0:128], in_=c.rB_pad[ei][kk * 128:(kk + 1) * 128, :],
                transpose=True), waits=(cast_done, p.dnow("rbmv")))
            p.dma(G, "rbmv", lambda e, ei=ei, kk=kk: e.dma_start(
                out=c.routedBcat[ei * R0:(ei + 1) * R0, kk * 128:(kk + 1) * 128],
                in_=rb_stage[0:R0, 0:128]), waits=(tpt,))

    mset = p.op(V, lambda e: e.memset(c.ones1[:], 1.0), inc=True)
    p.op(V, lambda e: e.memset(c.epsc[:], 1e-5), inc=True, drain=True)
    for nm, src in (("base_b", tn["L0_ffn_base_b"]), ("bg2", tn["L1_attn_bg2"]),
                    ("bog", tn["L1_attn_bog"]), ("scale1", tn["L1_attn_scale1"]),
                    ("scale2", tn["L1_attn_scale2"])):
        bl = p.dma(Y, "rbl", lambda e, src=src: e.dma_start(
            out=c.rowbuf[:], in_=src.unsqueeze(0)), waits=(p.now(T),))

        def bc_mm(e):
            e.matmul(c.bank(0), c.ones1[:], c.rowbuf[:, 0:512], start=True, stop=True)
            return e.matmul(c.bank(1), c.ones1[:], c.rowbuf[:, 512:1024], start=True,
                            stop=True)
        mm = p.op(T, bc_mm, inc=True, waits=(bl, mset, p.now(S)))
        p.op(S, lambda e, nm=nm: e.copy(out=c.bcast[nm][:, 0:512], in_=c.bank(0)),
             waits=(mm,))
        p.op(S, lambda e, nm=nm: e.copy(out=c.bcast[nm][:, 512:1024], in_=c.bank(1)),
             inc=True)
    p.dma(Y, "wld", lambda e: e.dma_start(out=c.bg1c[:],
                                          in_=tn["L1_attn_bg1"].unsqueeze(1)))
    rb_ld = p.dma(Y, "rbl", lambda e: e.dma_start(
        out=c.rowbuf[:, 0:E0], in_=tn["L0_ffn_router_b"].unsqueeze(0)),
        waits=(p.now(T),))
    mm = p.op(T, lambda e: e.matmul(c.psum[0:128, 0:8], c.ones1[:], c.rowbuf[:, 0:8],
                                    start=True, stop=True), inc=True,
              waits=(rb_ld, p.now(S)))
    p.op(S, lambda e: e.copy(out=c.rbc[:], in_=c.psum[0:128, 0:8]), inc=True,
         waits=(mm,))
    p.barrier((T, V, S))
    p.wait(T, p.dnow("tp"), p.dnow("wld"), p.dnow("stg"), p.dnow("rbmv"), p.dnow("tps"), p.dnow("tpr"))
    p.wait(V, p.dnow("tp"), p.dnow("wld"))
    p.wait(S, p.dnow("tp"), p.dnow("wld"), p.dnow("rbmv"))
    p.wait(G, p.dnow("tp"))
    return c



def ln_apply(c, v_ap, out_ap, kind="ln", scale_bc=None, eps=1e-5, rms_eps=1e-6, vwait=()):
    p = c.p
    i = c.nstat % 4
    c.nstat += 1
    sq = c.cols[:, i * 8:i * 8 + 1]
    sm = c.cols[:, i * 8 + 1:i * 8 + 2]
    v1 = c.cols[:, i * 8 + 2:i * 8 + 3]
    v2 = c.cols[:, i * 8 + 3:i * 8 + 4]
    v3 = c.cols[:, i * 8 + 4:i * 8 + 5]
    d0 = c.dump[:, 2, :]
    d1 = c.dump[:, 3, :]
    s1 = p.op(S, lambda e: e.activation(d0, v_ap, AF.Square, accum_out=sq), inc=True,
               waits=tuple(vwait) + (p.now(V),))
    if kind == "ln":
        s2 = p.op(S, lambda e: e.activation(d1, v_ap, AF.Copy, accum_out=sm), inc=True)
        p.op(V, lambda e: e.tensor_scalar(sm, sm, 1.0 / D, None, ALU.mult), drain=True,
             waits=(s2,))
        p.op(V, lambda e: e.tensor_tensor(v1, sm, sm, ALU.mult), drain=True)
        vv2 = p.op(V, lambda e: e.scalar_tensor_tensor(v2, sq, 1.0 / D, v1, ALU.mult,
                                                       ALU.subtract), drain=True,
                   inc=True)
        s3 = p.op(S, lambda e: e.activation(v2, v2, AF.Sqrt, bias=c.epsc[:]), inc=True,
                  waits=(vv2,))
        p.op(V, lambda e: e.reciprocal(v3, v2), drain=True, waits=(s3,))
        r2 = p.op(V, lambda e: e.scalar_tensor_tensor(v1, sm, -1.0, v3, ALU.mult,
                                                      ALU.mult), drain=True, inc=True)
        return p.op(V, lambda e: e.tensor_scalar(out_ap, v_ap, v3, v1, ALU.mult,
                                                 ALU.add), inc=True, drain=True,
                    waits=(r2,))
    else:
        s3 = p.op(S, lambda e: e.activation(v2, sq, AF.Sqrt), inc=True,
                  waits=(s1, p.now(V)))
        p.op(V, lambda e: e.tensor_scalar(v2, v2, 1.0 / np.sqrt(D), rms_eps, ALU.mult,
                                          ALU.add), drain=True, waits=(s3,))
        r1 = p.op(V, lambda e: e.reciprocal(v3, v2), drain=True, inc=True)
        sa = p.op(V, lambda e: e.tensor_scalar(d1, v_ap, v3, None, ALU.mult), inc=True,
                  drain=True, waits=(r1,))
        return p.op(V, lambda e: e.tensor_tensor(out_ap, d1, scale_bc, ALU.mult),
                    inc=True, waits=(sa,), drain=True)


def transpose_act(c, src_fn, dst):
    p = c.p
    prev = {}
    for tt in range(TT):
        for k in range(KT):
            bnk = (tt * KT + k) % 4
            w = (prev[bnk],) if bnk in prev else ()
            mm = p.op(T, lambda e, tt=tt, k=k, bnk=bnk: e.transpose(
                c.bank(bnk)[:, 0:128], src_fn(tt)[:, k * 128:(k + 1) * 128],
                c.identF[:]), inc=True, waits=w)
            prev[bnk] = p.op(S, lambda e, tt=tt, k=k, bnk=bnk: e.copy(
                out=dst[:, k, tt * 128:(tt + 1) * 128], in_=c.bank(bnk)[:, 0:128]),
                inc=True, waits=(mm,))
    p.barrier((T, S, V))


def softmax_topk(c, psum_ap, ncand, keepk, wout_ap, renorm, bias_bc=None, waits=()):
    p = c.p
    i = c.nstat % 4
    c.nstat += 1
    mx = c.cols[:, i * 8 + 5:i * 8 + 6]
    sm = c.cols[:, i * 8 + 6:i * 8 + 7]
    s2 = c.cols[:, i * 8 + 7:i * 8 + 8]
    lg = c.dump[:, 2, 0:8]
    cnt = c.dump[:, 3, 0:8]
    if bias_bc is not None:
        p.op(V, lambda e: e.tensor_tensor(lg[:, 0:ncand], psum_ap,
                                          bias_bc[:, 0:ncand], ALU.add), drain=True,
             waits=tuple(waits))
    else:
        p.op(V, lambda e: e.tensor_copy(lg[:, 0:ncand], psum_ap), drain=True,
             waits=tuple(waits))
    p.op(V, lambda e: e.tensor_reduce(mx, lg[:, 0:ncand], AX.X, ALU.max, negate=True),
         drain=True, inc=True)
    sx = p.op(S, lambda e: e.activation(lg[:, 0:ncand], lg[:, 0:ncand], AF.Exp,
                                        bias=mx, accum_out=sm), inc=True,
              waits=(p.now(V),))
    p.op(V, lambda e: e.reciprocal(sm, sm), drain=True, waits=(sx,))
    p.op(V, lambda e: e.tensor_scalar(lg[:, 0:ncand], lg[:, 0:ncand], sm, None,
                                      ALU.mult), drain=True)
    p.op(V, lambda e: e.memset(cnt[:, 0:ncand], 0.0), drain=True)
    for j in range(ncand):
        p.op(V, lambda e, j=j: e.scalar_tensor_tensor(
            cnt[:, 0:ncand], lg[:, 0:ncand], lg[:, j:j + 1], cnt[:, 0:ncand],
            ALU.is_lt, ALU.add), drain=True)
    p.op(V, lambda e: e.tensor_scalar(cnt[:, 0:ncand], cnt[:, 0:ncand],
                                      float(keepk) - 0.5, None, ALU.is_lt), drain=True)
    if ncand < 8:
        p.op(V, lambda e: e.memset(wout_ap[:, ncand:8], 0.0), drain=True)
    last = p.op(V, lambda e: e.tensor_tensor(wout_ap[:, 0:ncand], lg[:, 0:ncand],
                                             cnt[:, 0:ncand], ALU.mult), inc=True,
                drain=True)
    if renorm:
        p.op(V, lambda e: e.tensor_reduce(s2, wout_ap[:, 0:ncand], AX.X, ALU.add),
             drain=True)
        p.op(V, lambda e: e.tensor_scalar(s2, s2, 1e-6, None, ALU.add), drain=True)
        p.op(V, lambda e: e.reciprocal(s2, s2), drain=True)
        last = p.op(V, lambda e: e.tensor_scalar(wout_ap[:, 0:ncand],
                                                 wout_ap[:, 0:ncand], s2, None,
                                                 ALU.mult), inc=True, drain=True)
    return last



def step1(c, x0_ap, pf_ap, pr_ap):
    p = c.p
    for tt in range(TT):
        r0 = tt * 128
        lw = (p.now(V), p.now(S))
        l1 = p.dma(Y, "ld1", lambda e, r0=r0: e.dma_start(
            out=c.dump[:, 0, :], in_=pf_ap[r0:r0 + 128, :]), waits=lw)
        l2 = p.dma(Y, "ld1", lambda e, r0=r0: e.dma_start(
            out=c.dump[:, 1, :], in_=pr_ap[r0:r0 + 128, :]))
        l3 = p.dma(Y, "ld1", lambda e, r0=r0: e.dma_start(
            out=c.dump[:, 2, :], in_=x0_ap[r0:r0 + 128, :]))
        v = c.dump[:, 0, :]
        p.op(V, lambda e: e.tensor_tensor(v, v, c.dump[:, 1, :], ALU.add),
             waits=(l3,), drain=True)
        vv = p.op(V, lambda e: e.tensor_tensor(v, v, c.dump[:, 2, :], ALU.add),
                  inc=True, drain=True)
        ln_apply(c, v, c.resid[:, tt, :], kind="ln", vwait=(vv,))
    p.barrier((T, V, S))



def step2(c):
    p = c.p
    transpose_act(c, lambda tt: c.resid[:, tt, :], c.xT)
    for tt in range(TT):
        ts = tt * 128

        def mm_small(e, ts=ts):
            ins = None
            for k in range(KT):
                st, sp = (k == 0), (k == KT - 1)
                e.matmul(c.bank(4)[:, 0:16], c.xT[:, k, ts:ts + 128],
                         c.sharedAT[:, k, :], start=st, stop=sp)
                e.matmul(c.bank(6)[:, 0:96], c.xT[:, k, ts:ts + 128],
                         c.routedAT[:, k, :], start=st, stop=sp)
                ins = e.matmul(c.bank(5)[:, 0:8], c.xT[:, k, ts:ts + 128],
                               c.routerWT[:, k, 0:8], start=st, stop=sp)
            return ins
        mm = p.op(T, mm_small, inc=True, waits=(p.now(S), p.now(V)))
        softmax_topk(c, c.bank(5)[:, 0:E0], E0, 2, c.w8[:, tt, :], True, bias_bc=c.rbc,
                     waits=(mm,))
        hsb = c.gh_f.rearrange("p a b -> p (a b)")[:, 0:112]
        ev1 = p.op(S, lambda e, hsb=hsb: e.copy(out=hsb[:, 0:16], in_=c.bank(4)[:, 0:16]),
                   waits=(mm, p.now(V)))
        ev2 = p.op(S, lambda e, hsb=hsb: e.copy(out=hsb[:, 16:112],
                                                in_=c.bank(6)[:, 0:96]), inc=True)
        for ei in range(E0):
            hv = p.op(V, lambda e, ei=ei, hsb=hsb, tt=tt: e.tensor_scalar(
                hsb[:, 16 + ei * R0:16 + (ei + 1) * R0],
                hsb[:, 16 + ei * R0:16 + (ei + 1) * R0],
                c.w8[:, tt, ei:ei + 1], None, ALU.mult), drain=True, waits=(ev2,),
                inc=(ei == E0 - 1))
        mm2 = p.op(T, lambda e, hsb=hsb: e.transpose(c.bankB(7)[0:16, 0:128],
                                                     hsb[:, 0:16], c.identB[:]),
                   waits=(hv, p.now(S)))
        mm2b = p.op(T, lambda e, hsb=hsb: e.transpose(c.bankB(6)[0:96, 0:128],
                                                      hsb[:, 16:112], c.identB[:]),
                    inc=True)
        p.op(S, lambda e: e.copy(out=c.hTs[:], in_=c.bankB(7)[0:16, 0:128]),
             waits=(mm2b,))
        evT = p.op(S, lambda e: e.copy(out=c.hTr[:], in_=c.bankB(6)[0:96, 0:128]),
                   inc=True)

        def mm_main(e, ts=ts):
            ins = None
            for half in range(2):
                f0 = half * 512
                for k in range(KT):
                    e.matmul(c.bank(half), c.xT[:, k, ts:ts + 128],
                             c.wbufA[:, (half * KT + k) % 2, 0:512],
                             start=(k == 0), stop=False)
                e.matmul(c.bank(half), c.hT[0:16, 0:128], c.sharedBT[0:16, f0:f0 + 512],
                         start=False, stop=False)
                ins = e.matmul(c.bank(half), c.hT[16:112, 0:128],
                               c.routedBcat[:, f0:f0 + 512], start=False, stop=True)
            return ins
        ins_prev = None
        for half in range(2):
            f0 = half * 512
            for k in range(KT):
                slot = (half * KT + k) % 2
                wld = p.dma(Y, f"ws{slot}", lambda e, k=k, f0=f0, slot=slot: e.dma_start(
                    out=c.wbufA[:, slot, 0:512],
                    in_=c.baseWT[k * 128:(k + 1) * 128, f0:f0 + 512]),
                    waits=(p.now(T),))
                ins_prev = p.op(T, lambda e, k=k, half=half, ts=ts, slot=slot: e.matmul(
                    c.bank(half), c.xT[:, k, ts:ts + 128], c.wbufA[:, slot, 0:512],
                    start=(k == 0), stop=False), inc=True,
                    waits=(wld,) + ((evT,) if (half == 0 and k == 0) else ()))
            p.op(T, lambda e, half=half, f0=f0: e.matmul(
                c.bank(half), c.hTs[:], c.sharedBT[0:16, f0:f0 + 512],
                start=False, stop=False))
            p.op(T, lambda e, half=half, f0=f0: e.matmul(
                c.bank(half), c.hTr[:], c.routedBcat[:, f0:f0 + 512],
                start=False, stop=True), inc=True)
        mm3 = p.now(T)
        v = c.dump[:, 0, :]
        p.op(V, lambda e: e.tensor_tensor(v[:, 0:512], c.bank(0),
                                          c.bcast["base_b"][:, 0:512], ALU.add),
             waits=(mm3,))
        p.op(V, lambda e: e.tensor_tensor(v[:, 512:1024], c.bank(1),
                                          c.bcast["base_b"][:, 512:1024], ALU.add),
             drain=True)
        vv = p.op(V, lambda e, tt=tt: e.tensor_tensor(v, v, c.resid[:, tt, :], ALU.add),
                  inc=True, drain=True)
        ln_apply(c, v, c.resid[:, tt, :], kind="ln", vwait=(vv,))
    p.barrier((T, V, S))



def step3(c, upto3='all'):
    p = c.p
    if upto3 == 'rms1':
        for tt in range(TT):
            ln_apply(c, c.resid[:, tt, :], c.dacc[:, tt, :], kind="rms",
                     scale_bc=c.bcast["scale1"][:])
        p.barrier((T, V, S))
        return
    for tt in range(TT):
        ln_apply(c, c.resid[:, tt, :], c.dump[:, 1, :], kind="rms",
                 scale_bc=c.bcast["scale1"][:])
        for k in range(KT):
            bnk = k % 4
            mm = p.op(T, lambda e, k=k, bnk=bnk: e.transpose(
                c.bank(bnk)[:, 0:128], c.dump[:, 1, k * 128:(k + 1) * 128],
                c.identF[:]), inc=True, waits=(p.now(V), p.now(S)))
            p.op(S, lambda e, k=k, bnk=bnk, tt=tt: e.copy(
                out=c.xT[:, k, tt * 128:(tt + 1) * 128], in_=c.bank(bnk)[:, 0:128]),
                inc=True, waits=(mm,))
        p.barrier((T, V, S))
    for fc in range(4):
        f0 = fc * 512

        def mm_gh(e, f0=f0):
            ins = None
            for k in range(KT):
                ins = e.matmul(c.bank(4, 16), c.wg1T[:, k, :], c.xT[:, k, f0:f0 + 512],
                               start=(k == 0), stop=(k == KT - 1))
            return ins
        mm = p.op(T, mm_gh, inc=True, waits=(p.now(S),))
        p.op(S, lambda e, f0=f0: e.activation(c.ghT[:, f0:f0 + 512], c.bank(4, 16),
                                              AF.Silu, bias=c.bg1c[:]), inc=True,
             waits=(mm,))
    p.barrier((T, V, S))
    if upto3 == 'xt':
        p.op(V, lambda e: e.memset(c.dacc[:], 0.0), inc=True, drain=True)
        for k in range(KT):
            p.op(V, lambda e, k=k: e.tensor_copy(c.dacc[:, k, :],
                                                 c.xT[:, k, 0:1024]), inc=True,
                 drain=True)
        p.barrier((T, V, S))
        return
    p.op(V, lambda e: e.memset(c.dacc[:], 0.0), inc=True, drain=True)
    pairs = [(t, s) for t in range(4) for s in range(t + 1)]
    for sub in range(4):
        tiles = [sub + 4 * b for b in range(4)]
        for hh in range(2):
            h0 = hh * 512
            for bi in range(4):
                ts = tiles[bi] * 128
                mm = p.op(T, lambda e, bi=bi, ts=ts, h0=h0: e.matmul(
                    c.bank(bi), c.ghT[:, ts:ts + 128], c.wg2T[0:16, h0:h0 + 512],
                    start=True, stop=True), inc=True, waits=(p.now(S), p.now(V)))
                zf = c.dump[:, bi % 2, 0:512]
                zv = p.op(V, lambda e, bi=bi, zf=zf, h0=h0: e.tensor_tensor(
                    zf, c.bank(bi), c.bcast["bg2"][:, h0:h0 + 512], ALU.add),
                    inc=True, drain=True, waits=(mm,))
                p.op(S, lambda e, bi=bi, zf=zf: e.activation(
                    c.Pb[:, bi, :], zf, AF.Sigmoid, scale=1.0 / GLA_TEMP), inc=True,
                    waits=(zv,))
                se = p.op(S, lambda e, bi=bi, zf=zf: e.activation(
                    c.dump[:, 2 + bi % 2, 0:512], zf, AF.Exp, scale=-1.0 / GLA_TEMP),
                    inc=True)
                p.op(V, lambda e, bi=bi: e.tensor_scalar(
                    c.Pib[:, bi, :], c.dump[:, 2 + bi % 2, 0:512], 1.0, None, ALU.add),
                    inc=True, drain=True, waits=(se,))
            for bi in range(1, 4):
                p.op(V, lambda e, bi=bi: e.tensor_tensor(
                    c.Pb[:, bi, :], c.Pb[:, bi, :], c.Pb[:, bi - 1, :], ALU.mult),
                    drain=True)
            for bi in range(1, 4):
                p.op(V, lambda e, bi=bi: e.tensor_tensor(
                    c.Pib[:, bi, :], c.Pib[:, bi, :], c.Pib[:, bi - 1, :], ALU.mult),
                    drain=True, inc=(bi == 3))
            if upto3 == 'al1' and sub == DBG_SUB and hh == 0:
                p.barrier((T, V, S))
                return
            for wd, kind in ((c.wqT, "q"), (c.wkT, "k"), (c.wvT, "v"), (c.wogT, "og")):
                for k in range(KT):
                    slot = k % 2
                    wld = p.dma(Y, f"ws{slot}", lambda e, wd=wd, k=k, h0=h0,
                                slot=slot: e.dma_start(
                        out=c.wbufA[:, slot, 0:512],
                        in_=wd[k * 128:(k + 1) * 128, h0:h0 + 512]),
                        waits=(p.now(T),))

                    def mm_w(e, k=k, tiles=tiles):
                        ins = None
                        for bi in range(4):
                            ts = tiles[bi] * 128
                            ins = e.matmul(c.bank(4 + bi), c.xT[:, k, ts:ts + 128],
                                           c.wbufA[:, k % 2, 0:512], start=(k == 0),
                                           stop=(k == KT - 1))
                        return ins
                    last_mm = p.op(T, mm_w, inc=True,
                                   waits=(wld,) + ((p.now(S), p.now(V)) if k == 0 else ()))
                sil = {}
                for bi in range(4):
                    if kind == "q":
                        p.op(S, lambda e, bi=bi: e.copy(out=c.qb[:, bi, :],
                                                        in_=c.bank(4 + bi)), inc=True,
                             waits=(last_mm,))
                    elif kind == "k":
                        p.op(S, lambda e, bi=bi: e.copy(out=c.kb[:, bi, :],
                                                        in_=c.bank(4 + bi)), inc=True,
                             waits=(last_mm,))
                    elif kind == "v":
                        p.op(S, lambda e, bi=bi: e.copy(out=c.vb[:, bi, :],
                                                        in_=c.bank(4 + bi)), inc=True,
                             waits=(last_mm,))
                    else:
                        ov = p.op(V, lambda e, bi=bi, h0=h0: e.tensor_tensor(
                            c.dump[:, bi % 2, 0:512], c.bank(4 + bi),
                            c.bcast["bog"][:, h0:h0 + 512], ALU.add), inc=True,
                            drain=True,
                            waits=(last_mm,) + ((sil[bi - 2],) if bi - 2 in sil else ()))
                        sil[bi] = p.op(S, lambda e, bi=bi, h0=h0: e.activation(
                            c.og[:, bi, h0:h0 + 512], c.dump[:, bi % 2, 0:512],
                            AF.Silu), inc=True, waits=(ov,))
            p.barrier((T, V, S))
            if upto3 in ('q', 'k', 'v', 'og', 'alpha') and sub == DBG_SUB and hh == 0:
                src = {'q': c.qb, 'k': c.kb, 'v': c.vb, 'og': c.og, 'alpha': c.Pb}[upto3]
                wid = 1024 if upto3 == 'og' else 512
                for bi in range(4):
                    p.op(V, lambda e, bi=bi, src=src, wid=wid: e.tensor_copy(
                        c.dacc[:, bi, 0:wid], src[:, bi, 0:wid]), inc=True, drain=True)
                p.barrier((T, V, S))
                return
            for bi in range(4):
                pi = pairs.index((bi, bi))
                tm = c.dump[:, 2 + bi % 2, 0:512] if bi < 2 else c.dump[:, 2 + bi % 2, 512:1024]
                p.op(V, lambda e, bi=bi, tm=tm: e.tensor_tensor(
                    tm, c.qb[:, bi, :], c.kb[:, bi, :], ALU.mult), drain=True)
                p.op(V, lambda e, pi=pi, tm=tm: e.tensor_reduce(
                    c.sc[:, pi, :], tm.rearrange("p (h k) -> p h k", k=DK), AX.X,
                    ALU.add), drain=True, inc=True)
            for bi in range(4):
                p.op(V, lambda e, bi=bi: e.tensor_tensor(
                    c.qb[:, bi, :], c.qb[:, bi, :], c.Pb[:, bi, :], ALU.mult))
            for bi in range(4):
                p.op(V, lambda e, bi=bi: e.tensor_tensor(
                    c.kb[:, bi, :], c.kb[:, bi, :], c.Pib[:, bi, :], ALU.mult),
                    inc=(bi == 3))
            for pi, (t_, s_) in enumerate(pairs):
                if t_ == s_:
                    continue
                tm = c.dump[:, 2 + pi % 2, 0:512] if pi % 4 < 2 else c.dump[:, 2 + pi % 2, 512:1024]
                p.op(V, lambda e, t_=t_, s_=s_, tm=tm: e.tensor_tensor(
                    tm, c.qb[:, t_, :], c.kb[:, s_, :], ALU.mult), drain=True)
                p.op(V, lambda e, pi=pi, tm=tm: e.tensor_reduce(
                    c.sc[:, pi, :], tm.rearrange("p (h k) -> p h k", k=DK), AX.X,
                    ALU.add), drain=True, inc=True)
            p.op(V, lambda e: e.memset(c.oac[:], 0.0), drain=True)
            for pi, (t_, s_) in enumerate(pairs):
                tm = c.dump[:, 2 + pi % 2, 0:512] if pi % 4 < 2 else c.dump[:, 2 + pi % 2, 512:1024]
                scbc = c.sc[:, pi, :].unsqueeze(2).to_broadcast([128, 8, DK])
                p.op(V, lambda e, s_=s_, tm=tm, scbc=scbc: e.tensor_tensor(
                    tm.rearrange("p (h k) -> p h k", k=DK), scbc,
                    c.vb[:, s_, :].rearrange("p (h k) -> p h k", k=DK), ALU.mult),
                    drain=True)
                p.op(V, lambda e, t_=t_, tm=tm: e.tensor_tensor(
                    c.oac[:, t_, :], c.oac[:, t_, :], tm, ALU.add), drain=True)
            if upto3 == 'o' and sub == DBG_SUB and hh == 0:
                for bi in range(4):
                    p.op(V, lambda e, bi=bi: e.tensor_copy(
                        c.dacc[:, bi, 0:512], c.oac[:, bi, :]), inc=True, drain=True)
                p.barrier((T, V, S))
                return
            for bi in range(4):
                i8 = c.sc[:, 10, :]
                r8 = c.sc[:, 11, :]
                t8 = c.cols[:, 40:48]
                ov = c.oac[:, bi, :]
                o3 = ov.rearrange("p (h k) -> p h k", k=DK)
                p.op(V, lambda e, o3=o3, i8=i8: e.tensor_reduce(i8, o3, AX.X, ALU.add),
                     drain=True, inc=True)
                sq = c.dump[:, 2, 0:512]
                sqo = p.op(S, lambda e, ov=ov, sq=sq: e.activation(sq, ov, AF.Square),
                           inc=True, waits=(p.now(V),))
                p.op(V, lambda e, sq=sq, r8=r8: e.tensor_reduce(
                    r8, sq.rearrange("p (h k) -> p h k", k=DK), AX.X, ALU.add),
                    drain=True, waits=(sqo,))
                p.op(V, lambda e, i8=i8: e.tensor_scalar(i8, i8, 1.0 / DK, None,
                                                         ALU.mult), drain=True)
                p.op(V, lambda e, i8=i8, t8=t8: e.tensor_tensor(t8, i8, i8, ALU.mult),
                     drain=True)
                p.op(V, lambda e, r8=r8, t8=t8: e.scalar_tensor_tensor(
                    r8, r8, 1.0 / DK, t8, ALU.mult, ALU.subtract), drain=True)
                sr = p.op(S, lambda e, r8=r8: e.activation(r8, r8, AF.Sqrt, bias=c.epsc[:]),
                          inc=True, waits=(p.now(V),))
                p.op(V, lambda e, r8=r8: e.reciprocal(r8, r8), drain=True, waits=(sr,))
                p.op(V, lambda e, o3=o3, i8=i8: e.tensor_tensor(
                    o3, o3, i8.unsqueeze(2).to_broadcast([128, 8, DK]), ALU.subtract),
                    drain=True)
                p.op(V, lambda e, o3=o3, r8=r8: e.tensor_tensor(
                    o3, o3, r8.unsqueeze(2).to_broadcast([128, 8, DK]), ALU.mult),
                    drain=True)
                p.op(V, lambda e, bi=bi, ov=ov, h0=h0: e.tensor_tensor(
                    c.og[:, bi, h0:h0 + 512], ov, c.og[:, bi, h0:h0 + 512], ALU.mult),
                    inc=True, drain=True)
            p.barrier((V, S, T))
            if upto3 == 'oln' and sub == DBG_SUB and hh == 0:
                for bi in range(4):
                    p.op(V, lambda e, bi=bi: e.tensor_copy(
                        c.dacc[:, bi, 0:512], c.oac[:, bi, :]), inc=True, drain=True)
                p.barrier((T, V, S))
                return
        if upto3 == 'og2' and sub == DBG_SUB:
            for bi in range(4):
                p.op(V, lambda e, bi=bi: e.tensor_copy(
                    c.dacc[:, bi, :], c.og[:, bi, :]), inc=True, drain=True)
            p.barrier((T, V, S))
            return
        if upto3 == 'hh1' and sub == DBG_SUB:
            p.barrier((T, V, S))
            return
        for bi in range(4):
            for k in range(KT):
                bnk = k % 4
                mm = p.op(T, lambda e, bi=bi, k=k, bnk=bnk: e.transpose(
                    c.bankB(bnk)[:, 0:128], c.og[:, bi, k * 128:(k + 1) * 128],
                    c.identB[:]), inc=True, waits=(p.now(S), p.now(V)))
                p.op(S, lambda e, bi=bi, k=k, bnk=bnk: e.copy(
                    out=c.ogT[:, k, bi * 128:(bi + 1) * 128],
                    in_=c.bankB(bnk)[:, 0:128]), inc=True, waits=(mm,))
        p.barrier((T, S, V))
        for bi in range(4):
            tt = tiles[bi]
            for k in range(KT):
                slot = k % 2
                wld = p.dma(Y, f"ws{slot}", lambda e, k=k, slot=slot: e.dma_start(
                    out=c.wbufA[:, slot, :], in_=c.woT[k * 128:(k + 1) * 128, :]),
                    waits=(p.now(T),))

                def mm_wo(e, bi=bi, k=k, slot=slot):
                    e.matmul(c.bank(4), c.ogT[:, k, bi * 128:(bi + 1) * 128],
                             c.wbufA[:, slot, 0:512], start=(k == 0),
                             stop=(k == KT - 1))
                    return e.matmul(c.bank(5), c.ogT[:, k, bi * 128:(bi + 1) * 128],
                                    c.wbufA[:, slot, 512:1024], start=(k == 0),
                                    stop=(k == KT - 1))
                p.op(T, mm_wo, inc=True,
                     waits=(wld,) + ((p.now(V), p.now(S)) if k == 0 else ()))
            ev = p.op(V, lambda e, tt=tt: e.tensor_copy(c.dacc[:, tt, 0:512],
                                                        c.bank(4)), drain=True,
                      waits=(p.now(T),))
            p.op(V, lambda e, tt=tt: e.tensor_copy(c.dacc[:, tt, 512:1024],
                                                   c.bank(5)), inc=True, drain=True)
        if upto3 == 'wo1' and sub == DBG_SUB:
            p.barrier((T, V, S))
            return
    p.barrier((T, V, S))
    if upto3 == 'gla':
        return
    for tt in range(TT):
        v = c.dump[:, 1, :]
        vv = p.op(V, lambda e, tt=tt: e.tensor_tensor(v, c.resid[:, tt, :],
                                                      c.dacc[:, tt, :], ALU.add),
                  inc=True, drain=True, waits=(p.now(S),))
        ln_apply(c, v, c.dump[:, 0, :], kind="rms", scale_bc=c.bcast["scale2"][:],
                 vwait=(vv,))
        for k in range(KT):
            bnk = k % 4
            mm = p.op(T, lambda e, k=k, bnk=bnk: e.transpose(
                c.bank(bnk)[:, 0:128], c.dump[:, 0, k * 128:(k + 1) * 128],
                c.identF[:]), inc=True, waits=(p.now(V), p.now(S)))
            p.op(S, lambda e, k=k, bnk=bnk, tt=tt: e.copy(
                out=c.xT[:, k, tt * 128:(tt + 1) * 128], in_=c.bank(bnk)[:, 0:128]),
                inc=True, waits=(mm,))
        p.barrier((T, V, S))
    if upto3 == 'rms2':
        return
    for fc in range(DHID // 512):
        f0 = fc * 512
        for tt in range(TT):
            ts = tt * 128
            for k in range(KT):
                slot = k % 2
                wldg = p.dma(Y, f"ws{slot}", lambda e, k=k, f0=f0, slot=slot: e.dma_start(
                    out=c.wbufA[:, slot, 0:512],
                    in_=c.wgateT[k * 128:(k + 1) * 128, f0:f0 + 512]),
                    waits=(p.now(T),))
                wldu = p.dma(Y, f"wu{slot}", lambda e, k=k, f0=f0, slot=slot: e.dma_start(
                    out=c.wbufA[:, slot, 512:1024],
                    in_=c.wupT[k * 128:(k + 1) * 128, f0:f0 + 512]))

                def mm_gu(e, k=k, ts=ts, slot=slot):
                    e.matmul(c.bank(0), c.xT[:, k, ts:ts + 128],
                             c.wbufA[:, slot, 0:512], start=(k == 0),
                             stop=(k == KT - 1))
                    return e.matmul(c.bank(1), c.xT[:, k, ts:ts + 128],
                                    c.wbufA[:, slot, 512:1024], start=(k == 0),
                                    stop=(k == KT - 1))
                p.op(T, mm_gu, inc=True,
                     waits=(wldg, wldu) + ((p.now(S), p.now(V)) if k == 0 else ()))
            gu = p.now(T)
            sg = p.op(S, lambda e: e.activation(c.gh_f[:, 0, :], c.bank(0), AF.Silu),
                      inc=True, waits=(gu,))
            su = p.op(S, lambda e: e.copy(out=c.uh_f[:, 0, :], in_=c.bank(1)), inc=True)
            hv = p.op(V, lambda e: e.tensor_tensor(c.gh_f[:, 1, :], c.gh_f[:, 0, :],
                                                   c.uh_f[:, 0, :], ALU.mult),
                      inc=True, drain=True, waits=(su,))
            for k4 in range(4):
                bnk = 2 + k4 % 2
                mm = p.op(T, lambda e, k4=k4, bnk=bnk: e.transpose(
                    c.bankB(bnk)[:, 0:128], c.gh_f[:, 1, k4 * 128:(k4 + 1) * 128],
                    c.identB[:]), inc=True, waits=(hv, p.now(S)))
                p.op(S, lambda e, k4=k4, bnk=bnk: e.copy(
                    out=c.hT[:, k4 * 128:(k4 + 1) * 128],
                    in_=c.bankB(bnk)[:, 0:128]), inc=True, waits=(mm,))
            hTready = p.now(S)
            for k4 in range(4):
                slot = k4 % 2
                wld = p.dma(Y, f"wd{slot}", lambda e, k4=k4, f0=f0, slot=slot: e.dma_start(
                    out=c.wbufA[:, slot, :],
                    in_=c.wdownT[f0 + k4 * 128:f0 + (k4 + 1) * 128, :]),
                    waits=(p.now(T),))

                def mm_dn(e, k4=k4, slot=slot):
                    e.matmul(c.bank(4), c.hT[:, k4 * 128:(k4 + 1) * 128],
                             c.wbufA[:, slot, 0:512], start=(k4 == 0),
                             stop=(k4 == 3))
                    return e.matmul(c.bank(5), c.hT[:, k4 * 128:(k4 + 1) * 128],
                                    c.wbufA[:, slot, 512:1024], start=(k4 == 0),
                                    stop=(k4 == 3))
                p.op(T, mm_dn, inc=True,
                     waits=(wld, hTready) + ((p.now(V),) if k4 == 0 else ()))
            dn = p.now(T)
            p.op(V, lambda e, tt=tt: e.tensor_tensor(
                c.dacc[:, tt, 0:512], c.dacc[:, tt, 0:512], c.bank(4), ALU.add),
                drain=True, waits=(dn,))
            p.op(V, lambda e, tt=tt: e.tensor_tensor(
                c.dacc[:, tt, 512:1024], c.dacc[:, tt, 512:1024], c.bank(5), ALU.add),
                inc=True, drain=True)
    p.barrier((T, V, S))
    for tt in range(TT):
        v = c.dump[:, 1, :]
        vv = p.op(V, lambda e, tt=tt: e.scalar_tensor_tensor(
            v, c.resid[:, tt, :], 2.0, c.dacc[:, tt, :], ALU.mult, ALU.add),
            inc=True, drain=True, waits=(p.now(S),))
        ln_apply(c, v, c.resid[:, tt, :], kind="ln", vwait=(vv,))
    p.barrier((T, V, S))



def step4(c, out_ap):
    p = c.p
    transpose_act(c, lambda tt: c.resid[:, tt, :], c.xT)
    for tt in range(TT):
        ts = tt * 128

        def mm_r(e, ts=ts):
            ins = None
            for k in range(KT):
                ins = e.matmul(c.bank(6)[:, 0:8], c.xT[:, k, ts:ts + 128],
                               c.router1[:, k, :], start=(k == 0), stop=(k == KT - 1))
            return ins
        mm = p.op(T, mm_r, inc=True, waits=(p.now(S), p.now(V)))
        p.wait(V, mm)
        softmax_topk(c, c.bank(6)[:, 0:E1], E1, 6, c.w8[:, tt, :], False)
    p.barrier((T, V, S))
    p.op(V, lambda e: e.memset(c.dacc[:], 0.0), inc=True, drain=True)
    specs = []
    for ei in range(E1):
        specs.append(("exp", ei, c.f1gate[ei], c.f1up[ei], c.f1down[ei], F1))
    specs.append(("shr", 0, c.wsgate[:, 0:512], c.wsup[:, 0:512], c.wsdown[0:512, :], 512))
    specs.append(("shr", 1, c.wsgate[:, 512:1024], c.wsup[:, 512:1024],
                  c.wsdown[512:1024, :], 512))
    for kind, ei, wg, wu, wd, FF in specs:
        for tt in range(TT):
            ts = tt * 128
            for k in range(KT):
                slot = k % 2
                wldg = p.dma(Y, f"ws{slot}", lambda e, k=k, wg=wg, slot=slot: e.dma_start(
                    out=c.wbufA[:, slot, 0:FF], in_=wg[k * 128:(k + 1) * 128, :]),
                    waits=(p.now(T),))
                wldu = p.dma(Y, f"wu{slot}", lambda e, k=k, wu=wu, slot=slot: e.dma_start(
                    out=c.wbufA[:, slot, 512:512 + FF], in_=wu[k * 128:(k + 1) * 128, :]))

                def mm_gu(e, k=k, ts=ts, slot=slot, FF=FF):
                    e.matmul(c.bank(0)[:, 0:FF], c.xT[:, k, ts:ts + 128],
                             c.wbufA[:, slot, 0:FF], start=(k == 0), stop=(k == KT - 1))
                    return e.matmul(c.bank(1)[:, 0:FF], c.xT[:, k, ts:ts + 128],
                                    c.wbufA[:, slot, 512:512 + FF], start=(k == 0),
                                    stop=(k == KT - 1))
                p.op(T, mm_gu, inc=True,
                     waits=(wldg, wldu) + ((p.now(S), p.now(V)) if k == 0 else ()))
            gu = p.now(T)
            sg = p.op(S, lambda e, FF=FF: e.activation(c.gh_f[:, 0, 0:FF],
                                                       c.bank(0)[:, 0:FF], AF.Silu),
                      inc=True, waits=(gu,))
            su = p.op(S, lambda e, FF=FF: e.copy(out=c.uh_f[:, 0, 0:FF],
                                                 in_=c.bank(1)[:, 0:FF]), inc=True)
            hv = p.op(V, lambda e, FF=FF: e.tensor_tensor(
                c.gh_f[:, 1, 0:FF], c.gh_f[:, 0, 0:FF], c.uh_f[:, 0, 0:FF], ALU.mult),
                inc=True, drain=True, waits=(su,))
            if kind == "exp":
                hv = p.op(V, lambda e, ei=ei, tt=tt, FF=FF: e.tensor_scalar(
                    c.gh_f[:, 1, 0:FF], c.gh_f[:, 1, 0:FF], c.w8[:, tt, ei:ei + 1],
                    None, ALU.mult), inc=True, drain=True)
            for k4 in range(FF // 128):
                bnk = 2 + k4 % 2
                mm = p.op(T, lambda e, k4=k4, bnk=bnk: e.transpose(
                    c.bankB(bnk)[:, 0:128], c.gh_f[:, 1, k4 * 128:(k4 + 1) * 128],
                    c.identB[:]), inc=True, waits=(hv, p.now(S)))
                p.op(S, lambda e, k4=k4, bnk=bnk: e.copy(
                    out=c.hT[:, k4 * 128:(k4 + 1) * 128],
                    in_=c.bankB(bnk)[:, 0:128]), inc=True, waits=(mm,))
            hTready = p.now(S)
            for k4 in range(FF // 128):
                slot = k4 % 2
                wld = p.dma(Y, f"wd{slot}", lambda e, k4=k4, wd=wd, slot=slot: e.dma_start(
                    out=c.wbufA[:, slot, :], in_=wd[k4 * 128:(k4 + 1) * 128, :]),
                    waits=(p.now(T),))

                def mm_dn(e, k4=k4, slot=slot, FF=FF):
                    e.matmul(c.bank(4), c.hT[:, k4 * 128:(k4 + 1) * 128],
                             c.wbufA[:, slot, 0:512], start=(k4 == 0),
                             stop=(k4 == FF // 128 - 1))
                    return e.matmul(c.bank(5), c.hT[:, k4 * 128:(k4 + 1) * 128],
                                    c.wbufA[:, slot, 512:1024], start=(k4 == 0),
                                    stop=(k4 == FF // 128 - 1))
                p.op(T, mm_dn, inc=True,
                     waits=(wld, hTready) + ((p.now(V),) if k4 == 0 else ()))
            dn = p.now(T)
            p.op(V, lambda e, tt=tt: e.tensor_tensor(
                c.dacc[:, tt, 0:512], c.dacc[:, tt, 0:512], c.bank(4), ALU.add),
                drain=True, waits=(dn,))
            p.op(V, lambda e, tt=tt: e.tensor_tensor(
                c.dacc[:, tt, 512:1024], c.dacc[:, tt, 512:1024], c.bank(5), ALU.add),
                inc=True, drain=True)
    p.barrier((T, V, S))
    for tt in range(TT):
        v = c.dump[:, 1, :]
        vv = p.op(V, lambda e, tt=tt: e.tensor_tensor(v, c.resid[:, tt, :],
                                                      c.dacc[:, tt, :], ALU.add),
                  inc=True, drain=True, waits=(p.now(S),))
        lo = ln_apply(c, v, c.dump[:, 0, :], kind="ln", vwait=(vv,))
        p.dma(G, "st", lambda e, tt=tt: e.dma_start(
            out=out_ap[tt * 128:(tt + 1) * 128, :], in_=c.dump[:, 0, :]),
            waits=(lo,))
        p.wait(V, p.dnow("st"))
        p.wait(S, p.dnow("st"))
    p.wait(G, p.dnow("st"))
    p.op(G, lambda e: e.nop())



_MCFG = dict(L=4096, DM=1024, DI=2048, NS=16, DTR=64, Tc=512)

WNAMES = [
    "L0_ffn_base_W", "L0_ffn_base_b", "L0_ffn_shared_A", "L0_ffn_shared_B",
    "L0_ffn_routed_A", "L0_ffn_routed_B", "L0_ffn_router_W", "L0_ffn_router_b",
    "L1_attn_scale1", "L1_attn_scale2", "L1_attn_wq", "L1_attn_wk", "L1_attn_wv",
    "L1_attn_wg1", "L1_attn_bg1", "L1_attn_wg2", "L1_attn_bg2", "L1_attn_wog",
    "L1_attn_bog", "L1_attn_wo", "L1_attn_w_gate", "L1_attn_w_up", "L1_attn_w_down",
    "L1_ffn_router_w", "L1_ffn_w_gate", "L1_ffn_w_up", "L1_ffn_w_down",
    "L1_ffn_ws_gate", "L1_ffn_ws_up", "L1_ffn_ws_down",
]

_WSHAPES = {
    "L0_ffn_base_W": (1024, 1024), "L0_ffn_base_b": (1024,),
    "L0_ffn_shared_A": (16, 1024), "L0_ffn_shared_B": (1024, 16),
    "L0_ffn_routed_A": (6, 16, 1024), "L0_ffn_routed_B": (6, 1024, 16),
    "L0_ffn_router_W": (6, 1024), "L0_ffn_router_b": (6,),
    "L1_attn_scale1": (1024,), "L1_attn_scale2": (1024,),
    "L1_attn_wq": (1024, 1024), "L1_attn_wk": (1024, 1024),
    "L1_attn_wv": (1024, 1024), "L1_attn_wg1": (16, 1024),
    "L1_attn_bg1": (16,), "L1_attn_wg2": (1024, 16), "L1_attn_bg2": (1024,),
    "L1_attn_wog": (1024, 1024), "L1_attn_bog": (1024,),
    "L1_attn_wo": (1024, 1024), "L1_attn_w_gate": (4096, 1024),
    "L1_attn_w_up": (4096, 1024), "L1_attn_w_down": (1024, 4096),
    "L1_ffn_router_w": (1024, 8), "L1_ffn_w_gate": (8, 1024, 512),
    "L1_ffn_w_up": (8, 1024, 512), "L1_ffn_w_down": (8, 512, 1024),
    "L1_ffn_ws_gate": (1024, 1024), "L1_ffn_ws_up": (1024, 1024),
    "L1_ffn_ws_down": (1024, 1024),
}


def _mamba_host_layout(w, cfg):
    DTt = cfg["DI"] // 128
    out = {}
    out["x0"] = np.ascontiguousarray(w["x0"])
    for nm in ("in_proj", "out_proj", "x_proj", "dt_w"):
        out[nm] = np.ascontiguousarray(w[nm])
    out["A_log"] = np.ascontiguousarray(
        w["A_log"].reshape(DTt, 128, cfg["NS"]).transpose(1, 0, 2))
    out["dt_b"] = np.ascontiguousarray(w["dt_b"].reshape(DTt, 128).T)
    out["conv_b"] = np.ascontiguousarray(w["conv_b"].reshape(DTt, 128).T)
    out["D"] = np.ascontiguousarray(w["D"].reshape(DTt, 128).T)
    out["conv_w"] = np.ascontiguousarray(
        w["conv_w"].reshape(cfg["DI"], 4).reshape(DTt, 128, 4).transpose(1, 0, 2))
    sel = np.zeros((cfg["NS"], cfg["NS"] * 128), BF)
    for n in range(cfg["NS"]):
        sel[n, n * 128:(n + 1) * 128] = 1
    out["sel16"] = sel
    return out


def _build_launch1(cfg):
    nc = bass.Bass("TRN2", target_bir_lowering=False)
    L, DM, DI, NS_, DTR = cfg["L"], cfg["DM"], cfg["DI"], cfg["NS"], cfg["DTR"]
    DTt = DI // 128
    tens = {
        "x0": nc.dram_tensor("x0", [L, DM], F32, kind="ExternalInput").ap(),
        "in_proj": nc.dram_tensor("in_proj", [2 * DI, DM], F32, kind="ExternalInput").ap(),
        "out_proj": nc.dram_tensor("out_proj", [DM, DI], F32, kind="ExternalInput").ap(),
        "x_proj": nc.dram_tensor("x_proj", [DTR + 2 * NS_, DI], F32, kind="ExternalInput").ap(),
        "dt_w": nc.dram_tensor("dt_w", [DI, DTR], F32, kind="ExternalInput").ap(),
        "A_log": nc.dram_tensor("A_log", [128, DTt, NS_], F32, kind="ExternalInput").ap(),
        "dt_b": nc.dram_tensor("dt_b", [128, DTt], F32, kind="ExternalInput").ap(),
        "conv_b": nc.dram_tensor("conv_b", [128, DTt], F32, kind="ExternalInput").ap(),
        "D": nc.dram_tensor("D", [128, DTt], F32, kind="ExternalInput").ap(),
        "conv_w": nc.dram_tensor("conv_w", [128, DTt, 4], F32, kind="ExternalInput").ap(),
        "sel16": nc.dram_tensor("sel16", [NS_, NS_ * 128], BF16, kind="ExternalInput").ap(),
        "out": nc.dram_tensor("out", [L, DM], F32, kind="ExternalOutput").ap(),
    }
    p = Prog()
    stack = ExitStack()
    with stack:
        build_mamba_fwd(nc, p, stack, cfg, tens)
        p.build(nc, stack)
    return nc


def _build_launch2():
    nc = bass.Bass("TRN2", target_bir_lowering=False)
    tens = {}
    for nm in WNAMES:
        tens[nm] = nc.dram_tensor(nm, list(_WSHAPES[nm]), F32, kind="ExternalInput").ap()
    tens["x0s"] = nc.dram_tensor("x0s", [2048, 1024], F32, kind="ExternalInput").ap()
    tens["pfs"] = nc.dram_tensor("pfs", [2048, 1024], F32, kind="ExternalInput").ap()
    tens["prs"] = nc.dram_tensor("prs", [2048, 1024], F32, kind="ExternalInput").ap()
    tens["identF"] = nc.dram_tensor("identF", [128, 128], F32, kind="ExternalInput").ap()
    tens["identB"] = nc.dram_tensor("identB", [128, 128], BF16, kind="ExternalInput").ap()
    out = nc.dram_tensor("out", [2048, 1024], F32, kind="ExternalOutput").ap()
    p = Prog(inc_all=True)
    stack = ExitStack()
    with stack:
        c = alloc_common(nc, p, stack, tens)
        step1(c, tens["x0s"], tens["pfs"], tens["prs"])
        step2(c)
        step3(c)
        step4(c, out)
        p.build(nc, stack)
    return nc


def kernel(**inputs):
    inputs = {k: np.asarray(v, np.float32) for k, v in inputs.items()}
    x_full = inputs["L0_attn_hidden_states"]

    nc1 = _build_launch1(_MCFG)
    in_maps1 = []
    for cidx in range(8):
        b, d = cidx // 2, cidx % 2
        sfx = "f" if d == 0 else "r"
        w = dict(
            x0=x_full[b] if d == 0 else np.ascontiguousarray(x_full[b][::-1]),
            in_proj=inputs[f"L0_attn_in_proj_{sfx}"],
            conv_w=inputs[f"L0_attn_conv_w_{sfx}"],
            conv_b=inputs[f"L0_attn_conv_b_{sfx}"],
            x_proj=inputs[f"L0_attn_x_proj_{sfx}"],
            dt_w=inputs[f"L0_attn_dt_w_{sfx}"],
            dt_b=inputs[f"L0_attn_dt_b_{sfx}"],
            A_log=inputs[f"L0_attn_A_log_{sfx}"],
            D=inputs[f"L0_attn_D_{sfx}"],
            out_proj=inputs[f"L0_attn_out_proj_{sfx}"],
        )
        in_maps1.append(_mamba_host_layout(w, _MCFG))
    res1 = run_bass_kernel_spmd(nc1, in_maps1, core_ids=list(range(8)))
    pf = [np.asarray(res1.results[2 * b]["out"]) for b in range(4)]
    pr = [np.ascontiguousarray(np.asarray(res1.results[2 * b + 1]["out"])[::-1])
          for b in range(4)]

    nc2 = _build_launch2()
    identF = np.eye(128, dtype=np.float32)
    identB = np.eye(128, dtype=BF)
    in_maps2 = []
    for cidx in range(8):
        sl = slice(512 * cidx, 512 * (cidx + 1))
        im = {nm: inputs[nm] for nm in WNAMES}
        im["x0s"] = np.ascontiguousarray(
            np.concatenate([x_full[b, sl] for b in range(4)], 0))
        im["pfs"] = np.ascontiguousarray(
            np.concatenate([pf[b][sl] for b in range(4)], 0))
        im["prs"] = np.ascontiguousarray(
            np.concatenate([pr[b][sl] for b in range(4)], 0))
        im["identF"] = identF
        im["identB"] = identB
        in_maps2.append(im)
    res2 = run_bass_kernel_spmd(nc2, in_maps2, core_ids=list(range(8)))

    out = np.zeros((4, 4096, 1024), np.float32)
    for cidx in range(8):
        o = np.asarray(res2.results[cidx]["out"]).reshape(4, 512, 1024)
        out[:, 512 * cidx:512 * (cidx + 1), :] = o
    return out


# version 02
# speedup vs baseline: 1.7236x; 1.7236x over previous
import numpy as np
import ml_dtypes
import concourse.bass as bass
import concourse.mybir as mybir
from concourse.bass_utils import run_bass_kernel_spmd
from contextlib import ExitStack

"""Builders for the mk_seq_postln_d2 megakernel (dev module; inlined into kernel.py later)."""

F32 = mybir.dt.float32
BF16 = mybir.dt.bfloat16
AF = mybir.ActivationFunctionType
ALU = mybir.AluOpType
AX = mybir.AxisListType
BF = ml_dtypes.bfloat16
ENGINES = ("tensor", "vector", "scalar", "gpsimd", "sync")


class Prog:

    def __init__(self, auto_drain=False, inc_all=False):
        self.auto_drain = auto_drain
        self.inc_all = inc_all
        self.items = []
        self.ecnt = dict.fromkeys(ENGINES, 0)
        self.dcnt = {}

    def op(self, eng, fn, inc=False, drain=False, waits=()):
        if self.inc_all:
            inc = True
        self.items.append(("op", eng, fn, inc, drain, tuple(waits)))
        if inc:
            self.ecnt[eng] += 1
        return ("e", eng, self.ecnt[eng])

    def dma(self, eng, name, fn, waits=()):
        self.dcnt[name] = self.dcnt.get(name, 0) + 16
        self.items.append(("dma", eng, fn, name, tuple(waits)))
        return ("d", name, self.dcnt[name])

    def now(self, eng):
        return ("e", eng, self.ecnt[eng])

    def dnow(self, name):
        return ("d", name, self.dcnt.get(name, 0))

    def barrier(self, engs):
        snaps = [self.now(e) for e in engs]
        out = []
        for e in engs:
            ws = [s for s in snaps if s[1] != e]
            self.items.append(("waits", e, ws))
        return snaps

    def wait(self, eng, *tokens):
        self.items.append(("waits", eng, tuple(tokens)))

    def build(self, nc, stack, verbose=False):
        esem = {e: stack.enter_context(nc.semaphore(f"es_{e}")) for e in ENGINES}
        dsem = {n: stack.enter_context(nc.semaphore(f"ds_{n}")) for n in self.dcnt}
        block = stack.enter_context(nc.Block())
        per = {e: [] for e in ENGINES}
        for it in self.items:
            per[it[1]].append(it)

        def run_engine(e, eng_items):
            seen = {}

            def handle(engine):
                cnt = 0
                dc = {}
                for it in eng_items:
                    if it[0] == "waits":
                        for tk in it[2]:
                            _wait(engine, tk)
                        continue
                    if it[0] == "op":
                        _, _, fn, inc, drain, waits = it
                        for tk in waits:
                            _wait(engine, tk)
                        instr = fn(engine)
                        if drain or (self.auto_drain and e in ("vector", "scalar", "gpsimd")):
                            instr = engine.drain()
                        if inc:
                            cnt += 1
                            (instr if instr is not None else engine.nop()).then_inc(esem[e], 1)
                    else:
                        _, _, fn, name, waits = it
                        for tk in waits:
                            _wait(engine, tk)
                        d = fn(engine)
                        d.then_inc(dsem[name], 16)

            def _wait(engine, tk):
                kind, key, val = tk
                if val <= 0:
                    return
                if kind == "e" and key == e:
                    return
                prev = seen.get((kind, key), -1)
                if val <= prev:
                    return
                seen[(kind, key)] = val
                sem = esem[key] if kind == "e" else dsem[key]
                engine.wait_ge(sem, val)

            return handle

        for e in ENGINES:
            getattr(block, e if e != "sync" else "sync")(run_engine(e, per[e]))
        return esem, dsem


def ceil_div(a, b):
    return (a + b - 1) // b


def build_mamba_fwd(nc, p, stack, cfg, tens):
    L, DM, DI, NS, DTR, Tc = cfg["L"], cfg["DM"], cfg["DI"], cfg["NS"], cfg["DTR"], cfg["Tc"]
    NCH = L // Tc
    DTt = DI // 128
    KT = DM // 128
    MT = Tc // 128
    NP = DTR + 2 * NS

    sb = lambda n, s, d=F32: stack.enter_context(nc.sbuf_tensor(n, s, d))
    dram = lambda n, s, d=BF16: nc.dram_tensor(n, s, d, kind="Internal").ap()

    xbf = dram("xbf", [L, DM])
    inp_bf = dram("inp_bf", [2 * DI, DM])
    outp_bf = dram("outp_bf", [DM, DI])
    xp_bf = dram("xp_bf", [128, DI])
    dtw_bf = dram("dtw_bf", [DI, 128])
    outT_dram = dram("outT_dram", [DI, DM])

    in_projT = sb("in_projT", [128, KT, 2 * DI], BF16)
    x_projT = sb("x_projT", [128, DTt, 128], BF16)
    dt_wT = sb("dt_wT", [128, DI], BF16)
    sel_sb = sb("sel_sb", [NS, NS * 128], BF16)
    A_sb = sb("A_sb", [128, DTt, NS])
    Alog_sb = A_sb
    dtb_sb = sb("dtb_sb", [128, DTt])
    convb_sb = sb("convb_sb", [128, DTt])
    D_sb = sb("D_sb", [128, DTt])
    convw_sb = sb("convw_sb", [128, DTt, 4])
    x0cT = sb("x0cT", [128, KT, Tc], BF16)
    conv_in = sb("conv_in", [128, 4, Tc + 3], BF16)
    halo = sb("halo", [128, DTt, 3])
    x_silu = sb("x_silu", [128, DTt, Tc], BF16)
    delta = sb("delta", [128, DTt, Tc], BF16)
    u_bf = sb("u_bf", [128, DTt, Tc], BF16)
    y_acc = sb("y_acc", [128, DTt, Tc])
    carry = sb("carry", [128, DTt * NS])
    dtr_bf = sb("dtr_bf", [DTR, Tc], BF16)
    Btile = sb("Btile", [NS, Tc], BF16)
    Ctile = sb("Ctile", [NS, Tc], BF16)
    dA_b = sb("dA_b", [128, 4, Tc])
    dBu_b = sb("dBu_b", [128, 4, Tc], BF16)
    h_b = sb("h_b", [128, 4, Tc], BF16)
    m_b = sb("m_b", [128, 4, Tc], BF16)
    z_bf = m_b
    tmp_c = dBu_b
    yq = delta
    outT_t = sb("outT_t", [128, 2, DM], BF16)
    attn_sb = sb("attn_sb", [128, DM])
    sp_t = dA_b
    sp_r = sb("sp_r", [128, 4, Tc], BF16)
    stage_t = outT_t[:, 0, :]
    psum = stack.enter_context(nc.psum_tensor("psum", [128, 4096], F32))
    bank = lambda b, par=128: psum[0:par, b * 512:(b + 1) * 512]

    T, V, S, G, Y = "tensor", "vector", "scalar", "gpsimd", "sync"

    c1 = p.dma(G, "cast", lambda e: e.dma_start(out=xbf, in_=tens["x0"]))
    c2 = p.dma(G, "cast", lambda e: e.dma_start(out=inp_bf, in_=tens["in_proj"]))
    c3 = p.dma(G, "cast", lambda e: e.dma_start(out=outp_bf, in_=tens["out_proj"]))
    c4 = p.dma(G, "cast", lambda e: e.dma_start(out=xp_bf[0:NP, :], in_=tens["x_proj"]))
    c5 = p.dma(G, "cast", lambda e: e.dma_start(out=dtw_bf[:, 0:DTR], in_=tens["dt_w"]))
    cast_done = p.dnow("cast")
    for nm, dst in [("A_log", Alog_sb), ("dt_b", dtb_sb), ("conv_b", convb_sb),
                    ("D", D_sb), ("conv_w", convw_sb), ("sel16", sel_sb)]:
        src = tens[nm]
        p.dma(Y, "ld", lambda e, dst=dst, src=src: e.dma_start(out=dst[:], in_=src))
    ld_done = p.dnow("ld")
    for k in range(KT):
        p.dma(Y, "tp", lambda e, k=k: e.dma_start(
            out=in_projT[:, k, :], in_=inp_bf[:, k * 128:(k + 1) * 128], transpose=True),
            waits=(cast_done,))
    for dt in range(DTt):
        p.dma(Y, "tp", lambda e, dt=dt: e.dma_start(
            out=x_projT[:, dt, :], in_=xp_bf[:, dt * 128:(dt + 1) * 128], transpose=True),
            waits=(cast_done,))
    p.dma(Y, "tp", lambda e: e.dma_start(out=dt_wT[:], in_=dtw_bf, transpose=True),
          waits=(cast_done,))
    for kt in range(DTt):
        tpt = p.dma(Y, "tp", lambda e, kt=kt: e.dma_start(
            out=stage_t[:], in_=outp_bf[:, kt * 128:(kt + 1) * 128], transpose=True),
            waits=(cast_done, p.dnow("stg")))
        p.dma(G, "stg", lambda e, kt=kt: e.dma_start(
            out=outT_dram[kt * 128:(kt + 1) * 128, :], in_=stage_t[:]), waits=(tpt,))
    e1 = p.op(S, lambda e: e.activation(A_sb[:], Alog_sb[:], AF.Exp), inc=True,
              waits=(ld_done,))
    p.op(V, lambda e: e.tensor_scalar(A_sb[:], A_sb[:], -1.0, None, ALU.mult), inc=True,
         waits=(e1,), drain=True)
    p.op(V, lambda e: e.memset(halo[:], 0.0), inc=True)
    p.op(V, lambda e: e.memset(carry[:], 0.0), inc=True)
    prep_T = p.now(T); prep_V = p.now(V); prep_S = p.now(S)
    tp_done = p.dnow("tp")
    p.wait(T, tp_done, p.now(V), p.now(S))
    p.wait(V, tp_done)
    p.wait(S, tp_done)

    XB = (0, 1, 5, 6)
    for ch in range(NCH):
        t0 = ch * Tc
        xw = []
        for k in range(KT):
            xw.append(p.dma(Y, "x0c", lambda e, k=k, t0=t0: e.dma_start(
                out=x0cT[:, k, :], in_=xbf[t0:t0 + Tc, k * 128:(k + 1) * 128],
                transpose=True), waits=(p.now(V),)))
        x0c_ok = xw[-1]

        silu_snap = {}
        for g in range(0, DTt, 4):
            quad = (g, g + 1, g + 2, g + 3)
            mm_snap = {}
            for dt in quad:
                def mm_xz(e, dt=dt):
                    ins = None
                    for k in range(KT):
                        ins = e.matmul(bank(XB[dt % 4]),
                                       in_projT[:, k, dt * 128:(dt + 1) * 128],
                                       x0cT[:, k, :], start=(k == 0), stop=(k == KT - 1))
                    return ins
                mm_snap[dt] = p.op(T, mm_xz, inc=True,
                                   waits=(x0c_ok, silu_snap.get(g - 4, ("e", S, 0))))
            ev_snap = {}
            for dt in quad:
                ev_snap[dt] = p.op(S, lambda e, dt=dt: e.copy(
                    out=conv_in[:, dt % 4, 3:], in_=bank(XB[dt % 4])),
                    inc=True, waits=(mm_snap[dt],))
            for dt in quad:
                p.op(V, lambda e, dt=dt: e.tensor_copy(conv_in[:, dt % 4, 0:3], halo[:, dt, :]),
                     waits=(ev_snap[dt], silu_snap.get(g - 4, ("e", S, 0))))
            for j in range(4):
                for dt in quad:
                    if j == 0:
                        p.op(V, lambda e, dt=dt: e.tensor_scalar(
                            sp_t[:, dt % 4, :], conv_in[:, dt % 4, 0:Tc],
                            convw_sb[:, dt, 0:1], None, ALU.mult))
                    else:
                        p.op(V, lambda e, dt=dt, j=j: e.scalar_tensor_tensor(
                            sp_t[:, dt % 4, :], conv_in[:, dt % 4, j:Tc + j],
                            convw_sb[:, dt, j:j + 1], sp_t[:, dt % 4, :], ALU.mult, ALU.add))
            conv_v = {}
            for dt in quad:
                conv_v[dt] = p.op(V, lambda e, dt=dt: e.tensor_copy(
                    halo[:, dt, :], conv_in[:, dt % 4, Tc:Tc + 3]), inc=True)
            for dt in quad:
                silu_snap[g] = p.op(S, lambda e, dt=dt: e.activation(
                    x_silu[:, dt, :], sp_t[:, dt % 4, :], AF.Silu, bias=convb_sb[:, dt:dt + 1]),
                    inc=True, waits=(conv_v[dt],))
            for dt in quad:
                def mm_xdbl(e, dt=dt):
                    st = (dt == 0)
                    sp = (dt == DTt - 1)
                    e.matmul(bank(2, DTR), x_projT[:, dt, 0:DTR], x_silu[:, dt, :],
                             start=st, stop=sp)
                    e.matmul(bank(3, NS), x_projT[:, dt, DTR:DTR + NS], x_silu[:, dt, :],
                             start=st, stop=sp)
                    return e.matmul(bank(4, NS), x_projT[:, dt, DTR + NS:DTR + 2 * NS],
                                    x_silu[:, dt, :], start=st, stop=sp)
                p.op(T, mm_xdbl, inc=True, waits=(silu_snap[g],))
        xdbl_T = p.now(T)
        ev_dtr = p.op(S, lambda e: e.copy(out=dtr_bf[:], in_=bank(2, DTR)), inc=True,
                      waits=(xdbl_T,))
        ev_B = p.op(S, lambda e: e.copy(out=Btile[:], in_=bank(3, NS)), inc=True)
        ev_C = p.op(S, lambda e: e.copy(out=Ctile[:], in_=bank(4, NS)), inc=True)

        dmm = {}
        dv_hist = {}
        for g in range(0, DTt, 4):
            quad = (g, g + 1, g + 2, g + 3)
            for dt in quad:
                dmm[dt] = p.op(T, lambda e, dt=dt: e.matmul(
                    bank(XB[dt % 4]), dt_wT[0:DTR, dt * 128:(dt + 1) * 128], dtr_bf[:],
                    start=True, stop=True), inc=True,
                    waits=(ev_dtr, p.now(S)))
            sp_done = {}
            for dt in quad:
                p.op(S, lambda e, dt=dt: e.activation(
                    sp_t[:, dt % 4, 0:Tc], bank(XB[dt % 4]), AF.Abs, bias=dtb_sb[:, dt:dt + 1]),
                    waits=(dmm[dt], dv_hist.get(g - 4, ("e", V, 0))))
            for dt in quad:
                p.op(S, lambda e, dt=dt: e.activation(
                    sp_r[:, dt % 4, 0:Tc], bank(XB[dt % 4]),
                    AF.Relu, bias=dtb_sb[:, dt:dt + 1]))
            for dt in quad:
                p.op(S, lambda e, dt=dt: e.activation(
                    sp_t[:, dt % 4, 0:Tc], sp_t[:, dt % 4, 0:Tc], AF.Exp, scale=-1.0))
            for dt in quad:
                sp_done[dt] = p.op(S, lambda e, dt=dt: e.activation(
                    sp_t[:, dt % 4, 0:Tc], sp_t[:, dt % 4, 0:Tc], AF.Ln, bias=1.0), inc=True)
            dv = {}
            for dt in quad:
                dv[dt] = p.op(V, lambda e, dt=dt: e.tensor_tensor(
                    delta[:, dt, :], sp_r[:, dt % 4, 0:Tc],
                    sp_t[:, dt % 4, 0:Tc], ALU.add),
                    inc=True, waits=(sp_done[dt],))
            for dt in quad:
                p.op(V, lambda e, dt=dt: e.tensor_tensor(
                    u_bf[:, dt, :], delta[:, dt, :], x_silu[:, dt, :], ALU.mult), inc=True)
            dv_hist[g] = p.now(V)
        yz = p.op(V, lambda e: e.memset(y_acc[:], 0.0), inc=True)
        p.barrier((T, V, S))

        scan_v_snap = {}
        n_last_v = {}
        for n in range(NS):
            bB = 2 if n % 2 == 0 else 3
            bC = 7 if n % 2 == 0 else 4
            war = n_last_v.get(n - 2, ("e", V, 0))
            selL = sel_sb[:, n * 128:(n + 1) * 128]
            bc1 = p.op(T, lambda e, bB=bB, selL=selL: e.matmul(
                bank(bB), selL, Btile[:], start=True, stop=True), waits=(ev_B, ev_C, war))
            bc2 = p.op(T, lambda e, bC=bC, selL=selL: e.matmul(
                bank(bC), selL, Ctile[:], start=True, stop=True), inc=True)
            for g in range(0, DTt, 4):
                quad = (g, g + 1, g + 2, g + 3)
                da = {}
                for dt in quad:
                    prev = (scan_v_snap.get((n, dt - 4))
                            or scan_v_snap.get((n - 1, DTt - 4 + (dt % 4)))
                            or ("e", V, 0))
                    da[dt] = p.op(S, lambda e, dt=dt, n=n: e.activation(
                        dA_b[:, dt % 4, :], delta[:, dt, :], AF.Exp,
                        scale=A_sb[:, dt, n:n + 1]), inc=True, waits=(prev,))
                for dt in quad:
                    p.op(V, lambda e, dt=dt, bB=bB: e.tensor_tensor(
                        dBu_b[:, dt % 4, :], u_bf[:, dt, :], bank(bB), ALU.mult),
                        waits=(da[dt], bc2))
                for dt in quad:
                    scan_v_snap[(n, dt)] = p.op(V, lambda e, dt=dt, n=n: e.tensor_tensor_scan(
                        h_b[:, dt % 4, :], dA_b[:, dt % 4, :], dBu_b[:, dt % 4, :],
                        carry[:, dt * NS + n:dt * NS + n + 1], ALU.mult, ALU.add), inc=True)
                for dt in quad:
                    p.op(V, lambda e, dt=dt, bC=bC: e.tensor_tensor(
                        m_b[:, dt % 4, :], h_b[:, dt % 4, :], bank(bC), ALU.mult))
                for dt in quad:
                    p.op(V, lambda e, dt=dt, n=n: e.tensor_copy(
                        carry[:, dt * NS + n:dt * NS + n + 1], h_b[:, dt % 4, Tc - 1:Tc]))
                for dt in quad:
                    p.op(V, lambda e, dt=dt: e.tensor_tensor(
                        y_acc[:, dt, :], y_acc[:, dt, :], m_b[:, dt % 4, :], ALU.add),
                        inc=(dt == DTt - 1))
            n_last_v[n] = p.now(V)
        p.barrier((T, V, S))

        for g in range(0, DTt, 4):
            quad = (g, g + 1, g + 2, g + 3)
            zmm = {}
            for dt in quad:
                def mm_z(e, dt=dt):
                    ins = None
                    for k in range(KT):
                        ins = e.matmul(bank(XB[dt % 4]),
                                       in_projT[:, k, (DI + dt * 128):(DI + (dt + 1) * 128)],
                                       x0cT[:, k, :], start=(k == 0), stop=(k == KT - 1))
                    return ins
                zmm[dt] = p.op(T, mm_z, inc=True, waits=(p.now(V), p.now(S)))
            zs = {}
            for dt in quad:
                zs[dt] = p.op(S, lambda e, dt=dt: e.activation(
                    z_bf[:, dt % 4, :], bank(XB[dt % 4]), AF.Silu), inc=True, waits=(zmm[dt],))
            for dt in quad:
                p.op(V, lambda e, dt=dt: e.tensor_scalar(
                    tmp_c[:, dt % 4, :], x_silu[:, dt, :], D_sb[:, dt:dt + 1], None, ALU.mult),
                    waits=(zs[dt],))
            for dt in quad:
                p.op(V, lambda e, dt=dt: e.tensor_tensor(
                    tmp_c[:, dt % 4, :], y_acc[:, dt, :], tmp_c[:, dt % 4, :], ALU.add))
            for dt in quad:
                p.op(V, lambda e, dt=dt: e.tensor_tensor(
                    yq[:, dt, :], tmp_c[:, dt % 4, :], z_bf[:, dt % 4, :], ALU.mult), inc=True)
        p.barrier((T, V, S))

        dmm2 = {}
        for dt in range(DTt):
            lw = (dmm2[dt - 2],) if dt - 2 in dmm2 else (p.dnow("stg"),)
            otw = p.dma(Y, f"outT{dt % 2}", lambda e, dt=dt: e.dma_start(
                out=outT_t[:, dt % 2, :], in_=outT_dram[dt * 128:(dt + 1) * 128, :]),
                waits=lw)

            def mm_out(e, dt=dt):
                st = (dt == 0)
                sp = (dt == DTt - 1)
                ins = None
                for m in range(MT):
                    for hh in range(2):
                        ins = e.matmul(bank(m * 2 + hh), yq[:, dt, m * 128:(m + 1) * 128],
                                       outT_t[:, dt % 2, hh * 512:(hh + 1) * 512],
                                       start=st, stop=sp)
                return ins
            dmm2[dt] = p.op(T, mm_out, inc=True, waits=(otw,))
        out_T = p.now(T)
        for m in range(MT):
            evm = p.op(S, lambda e, m=m: e.copy(out=attn_sb[:], in_=psum[:, m * 1024:(m + 1) * 1024]),
                       inc=True, waits=(out_T, p.dnow("st")))
            p.dma(G, "st", lambda e, m=m, t0=t0: e.dma_start(
                out=tens["out"][t0 + m * 128: t0 + (m + 1) * 128, :], in_=attn_sb[:]),
                waits=(evm,))
        p.barrier((T, V, S, G, Y))
    p.wait(G, p.dnow("st"))
    p.op(G, lambda e: e.nop())
    return



"""Phase-B builders: x1=LN(x0+pf+pr); L0_ffn; L1_attn (GLA block); L1_ffn; final LN.

Token layout per core: 2048 tokens = 4 batches x 512 seq positions, b-major
(row = b*512 + l_local). 16 token-tiles of 128. GLA "time" = batch axis (4
steps); sub s (s=0..3) covers tiles {s, 4+s, 8+s, 12+s} = same l-range, b=0..3.
Canonical activation layout: (token-part 128, feature-free) f32 residual.
Matmul inputs are transposed on demand via TensorE into xT (feature-part,
token-free) bf16.

PSUM rule: one accumulation group per bank (first_mm clears the whole bank).
Same-engine dependent V/S ops carry drain=True (v1; modest op counts here).
"""

F32 = mybir.dt.float32
BF16 = mybir.dt.bfloat16
AF = mybir.ActivationFunctionType
ALU = mybir.AluOpType
AX = mybir.AxisListType
BF = ml_dtypes.bfloat16

T, V, S, G, Y = "tensor", "vector", "scalar", "gpsimd", "sync"

D = 1024
TT = 16
KT = 8
N = 2048
NH = 16
DK = 64
DHID = 4096
E0, R0 = 6, 16
E1, F1, FS = 8, 512, 1024
GLA_TEMP = 16.0
DBG_SUB = 0


class Ctx:
    pass


def alloc_common(nc, p, stack, tens):
    c = Ctx()
    c.nc, c.p, c.stack, c.tens = nc, p, stack, tens
    sb = lambda n, s, d=F32: stack.enter_context(nc.sbuf_tensor(n, s, d))
    dram = lambda n, s, d=BF16: nc.dram_tensor(n, s, d, kind="Internal").ap()
    c.dram = dram

    c.resid = sb("resid", [128, TT, D])
    c.dacc = sb("dacc", [128, TT, D], BF16)
    c.xT = sb("xT", [128, KT, N], BF16)
    c.dump = sb("dump", [128, 4, D])
    c.cols = sb("cols", [128, 64])
    c.arena = sb("arena", [128, 24, 512], BF16)
    c.qkv = c.arena[:, 0:12, :]
    c.qb = c.qkv[:, 0:4, :]
    c.kb = c.qkv[:, 4:8, :]
    c.vb = c.qkv[:, 8:12, :]
    c.ogT = c.arena[:, 0:8, :]
    c.og = c.arena[:, 12:20, :].rearrange("p (a b) c -> p a (b c)", b=2)
    c.slab_g = c.arena[:, 0:8, :]
    c.slab_u = c.arena[:, 8:16, :]
    c.slab_d = c.arena[:, 16:24, :].rearrange("p (a b) c -> p a (b c)", b=2)
    c.Pb = sb("Pb", [128, 4, 512], BF16)
    c.Pib = sb("Pib", [128, 4, 512], BF16)
    c.oac = None
    c.sc = sb("sc", [128, 12, 8])
    c.u1 = sb("u1", [128, 4, 512], BF16)
    c.gh_f = c.u1[:, 0:2, :]
    c.uh_f = c.u1[:, 2:4, :]
    c.ghT = c.u1[0:16].rearrange("p a b -> p (a b)")
    c.hT = sb("hT", [128, 512], BF16)
    c.hTs = sb("hTs", [16, 128], BF16)
    c.hTr = sb("hTr", [96, 128], BF16)
    c.w8 = sb("w8", [128, TT, 8])
    c.wbufA = sb("wbufA", [128, 2, 1024], BF16)
    c.wg1T = sb("wg1T", [128, KT, 16], BF16)
    c.wg2T = sb("wg2T", [128, D], BF16)
    c.sharedAT = sb("sharedAT", [128, KT, 16], BF16)
    c.sharedBT = sb("sharedBT", [128, D], BF16)
    c.routedAT = sb("routedAT", [128, KT, E0 * R0], BF16)
    c.routedBcat = sb("routedBcat", [96, D], BF16)
    c.routerWT = sb("routerWT", [128, KT, 16], BF16)
    c.router1 = sb("router1", [128, KT, 8], BF16)
    c.bg1c = sb("bg1c", [16, 1])
    c.rbc = sb("rbc", [128, 8])
    c.bcast = {}
    for nm in ("base_b", "bg2", "bog", "scale1", "scale2"):
        c.bcast[nm] = sb(f"bc_{nm}", [128, D], BF16)
    c.ones1 = sb("ones1", [1, 128])
    c.rowbuf = None
    c.identF = sb("identF_sb", [128, 128])
    c.epsc = sb("epsc", [128, 1])
    c.identB = sb("identB_sb", [128, 128], BF16)
    c.psum = stack.enter_context(nc.psum_tensor("psumB", [128, 4096], F32))
    c.bank = lambda b, par=128: c.psum[0:par, b * 512:(b + 1) * 512]
    _psB = c.psum.bitcast(BF16)
    c.bankB = lambda b, par=128: _psB[0:par, b * 1024:(b + 1) * 1024]
    c.nstat = 0
    c.rowbuf = c.dump[0:1, 0, :]
    c.oac = c.dump[:, 0:2, :].rearrange("p a (b f) -> p (a b) f", f=512)

    p.dma(Y, "wld", lambda e: e.dma_start(out=c.identF[:], in_=tens["identF"]))
    p.dma(Y, "wld", lambda e: e.dma_start(out=c.identB[:], in_=tens["identB"]))

    c.wdram = {}

    def cast(nm, src, shape):
        dd = dram(f"cst_{nm}", shape)
        p.dma(G, "cast", lambda e, dd=dd, src=src: e.dma_start(out=dd, in_=src))
        c.wdram[nm] = dd
        return dd

    c.trans_specs = []

    def prep_transposed(nm, src, rows, cols):
        raw = cast(nm + "_raw", src, [rows, cols])
        out = dram(nm + "T", [cols, rows])
        c.trans_specs.append((nm, raw, out, rows, cols))
        return out

    tn = tens
    c.baseWT = prep_transposed("baseW", tn["L0_ffn_base_W"], D, D)
    c.wqT = prep_transposed("wq", tn["L1_attn_wq"], D, D)
    c.wkT = prep_transposed("wk", tn["L1_attn_wk"], D, D)
    c.wvT = prep_transposed("wv", tn["L1_attn_wv"], D, D)
    c.wogT = prep_transposed("wog", tn["L1_attn_wog"], D, D)
    c.woT = prep_transposed("wo", tn["L1_attn_wo"], D, D)
    c.wgateT = prep_transposed("wgate", tn["L1_attn_w_gate"], DHID, D)
    c.wupT = prep_transposed("wup", tn["L1_attn_w_up"], DHID, D)
    c.wdownT = prep_transposed("wdown", tn["L1_attn_w_down"], D, DHID)
    c.f1gate = cast("f1gate", tn["L1_ffn_w_gate"], [E1, D, F1])
    c.f1up = cast("f1up", tn["L1_ffn_w_up"], [E1, D, F1])
    c.f1down = cast("f1down", tn["L1_ffn_w_down"], [E1, F1, D])
    c.wsgate = cast("wsgate", tn["L1_ffn_ws_gate"], [D, FS])
    c.wsup = cast("wsup", tn["L1_ffn_ws_up"], [D, FS])
    c.wsdown = cast("wsdown", tn["L1_ffn_ws_down"], [FS, D])
    c.sA_raw = cast("sA_raw", tn["L0_ffn_shared_A"], [R0, D])
    c.wg1_raw = cast("wg1_raw", tn["L1_attn_wg1"], [16, D])
    c.rW_raw = dram("rW_raw", [16, D])
    p.dma(G, "cast", lambda e: e.dma_start(out=c.rW_raw[0:E0, :],
                                           in_=tn["L0_ffn_router_W"]))
    c.wg2_pad = dram("wg2_pad", [D, 128])
    p.dma(G, "cast", lambda e: e.dma_start(out=c.wg2_pad[:, 0:16],
                                           in_=tn["L1_attn_wg2"]))
    c.sB_pad = dram("sB_pad", [D, 128])
    p.dma(G, "cast", lambda e: e.dma_start(out=c.sB_pad[:, 0:R0],
                                           in_=tn["L0_ffn_shared_B"]))
    c.rA_raw = []
    c.rB_pad = []
    for ei in range(E0):
        ra = dram(f"rA_raw{ei}", [R0, D])
        p.dma(G, "cast", lambda e, ra=ra, ei=ei: e.dma_start(
            out=ra, in_=tn["L0_ffn_routed_A"][ei]))
        c.rA_raw.append(ra)
        rb = dram(f"rB_pad{ei}", [D, 128])
        p.dma(G, "cast", lambda e, rb=rb, ei=ei: e.dma_start(
            out=rb[:, 0:R0], in_=tn["L0_ffn_routed_B"][ei]))
        c.rB_pad.append(rb)
    c.r1_raw = cast("r1_raw", tn["L1_ffn_router_w"], [D, 8])
    cast_done = p.dnow("cast")

    stg = c.wbufA[:, 0, :]
    for nm, raw, out, rows, cols in c.trans_specs:
        for kk in range(cols // 128):
            for r0 in range(0, rows, 1024):
                rc = min(1024, rows - r0)
                tpt = p.dma(Y, "tps", lambda e, raw=raw, kk=kk, r0=r0, rc=rc: e.dma_start(
                    out=stg[:, 0:rc], in_=raw[r0:r0 + rc, kk * 128:(kk + 1) * 128],
                    transpose=True), waits=(cast_done, p.dnow("stg")))
                p.dma(G, "stg", lambda e, out=out, kk=kk, r0=r0, rc=rc: e.dma_start(
                    out=out[kk * 128:(kk + 1) * 128, r0:r0 + rc], in_=stg[:, 0:rc]),
                    waits=(tpt,))
    for kk in range(KT):
        p.dma(Y, "tp", lambda e, kk=kk: e.dma_start(
            out=c.sharedAT[:, kk, :], in_=c.sA_raw[:, kk * 128:(kk + 1) * 128],
            transpose=True), waits=(cast_done,))
        p.dma(Y, "tp", lambda e, kk=kk: e.dma_start(
            out=c.wg1T[:, kk, :], in_=c.wg1_raw[:, kk * 128:(kk + 1) * 128],
            transpose=True), waits=(cast_done,))
        p.dma(Y, "tp", lambda e, kk=kk: e.dma_start(
            out=c.routerWT[:, kk, :], in_=c.rW_raw[:, kk * 128:(kk + 1) * 128],
            transpose=True), waits=(cast_done,))
        for ei in range(E0):
            p.dma(Y, "tp", lambda e, kk=kk, ei=ei: e.dma_start(
                out=c.routedAT[:, kk, ei * R0:(ei + 1) * R0],
                in_=c.rA_raw[ei][:, kk * 128:(kk + 1) * 128], transpose=True),
                waits=(cast_done,))
        p.dma(Y, "tp", lambda e, kk=kk: e.dma_start(
            out=c.wg2T[:, kk * 128:(kk + 1) * 128],
            in_=c.wg2_pad[kk * 128:(kk + 1) * 128, :], transpose=True),
            waits=(cast_done,))
        p.dma(Y, "tp", lambda e, kk=kk: e.dma_start(
            out=c.sharedBT[:, kk * 128:(kk + 1) * 128],
            in_=c.sB_pad[kk * 128:(kk + 1) * 128, :], transpose=True),
            waits=(cast_done,))
        p.dma(Y, "wld", lambda e, kk=kk: e.dma_start(
            out=c.router1[:, kk, :], in_=c.r1_raw[kk * 128:(kk + 1) * 128, :]),
            waits=(cast_done,))
    rb_stage = c.dacc[:, 0, :]
    for ei in range(E0):
        for kk in range(KT):
            tpt = p.dma(Y, "tpr", lambda e, ei=ei, kk=kk: e.dma_start(
                out=rb_stage[:, 0:128], in_=c.rB_pad[ei][kk * 128:(kk + 1) * 128, :],
                transpose=True), waits=(cast_done, p.dnow("rbmv")))
            p.dma(G, "rbmv", lambda e, ei=ei, kk=kk: e.dma_start(
                out=c.routedBcat[ei * R0:(ei + 1) * R0, kk * 128:(kk + 1) * 128],
                in_=rb_stage[0:R0, 0:128]), waits=(tpt,))

    mset = p.op(V, lambda e: e.memset(c.ones1[:], 1.0), inc=True)
    p.op(V, lambda e: e.memset(c.epsc[:], 1e-5), inc=True, drain=True)
    for nm, src in (("base_b", tn["L0_ffn_base_b"]), ("bg2", tn["L1_attn_bg2"]),
                    ("bog", tn["L1_attn_bog"]), ("scale1", tn["L1_attn_scale1"]),
                    ("scale2", tn["L1_attn_scale2"])):
        bl = p.dma(Y, "rbl", lambda e, src=src: e.dma_start(
            out=c.rowbuf[:], in_=src.unsqueeze(0)), waits=(p.now(T),))

        def bc_mm(e):
            e.matmul(c.bank(0), c.ones1[:], c.rowbuf[:, 0:512], start=True, stop=True)
            return e.matmul(c.bank(1), c.ones1[:], c.rowbuf[:, 512:1024], start=True,
                            stop=True)
        mm = p.op(T, bc_mm, inc=True, waits=(bl, mset, p.now(S)))
        p.op(S, lambda e, nm=nm: e.copy(out=c.bcast[nm][:, 0:512], in_=c.bank(0)),
             waits=(mm,))
        p.op(S, lambda e, nm=nm: e.copy(out=c.bcast[nm][:, 512:1024], in_=c.bank(1)),
             inc=True)
    p.dma(Y, "wld", lambda e: e.dma_start(out=c.bg1c[:],
                                          in_=tn["L1_attn_bg1"].unsqueeze(1)))
    rb_ld = p.dma(Y, "rbl", lambda e: e.dma_start(
        out=c.rowbuf[:, 0:E0], in_=tn["L0_ffn_router_b"].unsqueeze(0)),
        waits=(p.now(T),))
    mm = p.op(T, lambda e: e.matmul(c.psum[0:128, 0:8], c.ones1[:], c.rowbuf[:, 0:8],
                                    start=True, stop=True), inc=True,
              waits=(rb_ld, p.now(S)))
    p.op(S, lambda e: e.copy(out=c.rbc[:], in_=c.psum[0:128, 0:8]), inc=True,
         waits=(mm,))
    p.barrier((T, V, S))
    p.wait(T, p.dnow("tp"), p.dnow("wld"), p.dnow("stg"), p.dnow("rbmv"), p.dnow("tps"), p.dnow("tpr"))
    p.wait(V, p.dnow("tp"), p.dnow("wld"))
    p.wait(S, p.dnow("tp"), p.dnow("wld"), p.dnow("rbmv"))
    p.wait(G, p.dnow("tp"))
    return c



def ln_apply(c, v_ap, out_ap, kind="ln", scale_bc=None, eps=1e-5, rms_eps=1e-6, vwait=()):
    p = c.p
    i = c.nstat % 4
    c.nstat += 1
    sq = c.cols[:, i * 8:i * 8 + 1]
    sm = c.cols[:, i * 8 + 1:i * 8 + 2]
    v1 = c.cols[:, i * 8 + 2:i * 8 + 3]
    v2 = c.cols[:, i * 8 + 3:i * 8 + 4]
    v3 = c.cols[:, i * 8 + 4:i * 8 + 5]
    d0 = c.dump[:, 2, :]
    d1 = c.dump[:, 3, :]
    s1 = p.op(S, lambda e: e.activation(d0, v_ap, AF.Square, accum_out=sq), inc=True,
               waits=tuple(vwait) + (p.now(V),))
    if kind == "ln":
        s2 = p.op(S, lambda e: e.activation(d1, v_ap, AF.Copy, accum_out=sm), inc=True)
        p.op(V, lambda e: e.tensor_scalar(sm, sm, 1.0 / D, None, ALU.mult), drain=True,
             waits=(s2,))
        p.op(V, lambda e: e.tensor_tensor(v1, sm, sm, ALU.mult), drain=True)
        vv2 = p.op(V, lambda e: e.scalar_tensor_tensor(v2, sq, 1.0 / D, v1, ALU.mult,
                                                       ALU.subtract), drain=True,
                   inc=True)
        s3 = p.op(S, lambda e: e.activation(v2, v2, AF.Sqrt, bias=c.epsc[:]), inc=True,
                  waits=(vv2,))
        p.op(V, lambda e: e.reciprocal(v3, v2), drain=True, waits=(s3,))
        r2 = p.op(V, lambda e: e.scalar_tensor_tensor(v1, sm, -1.0, v3, ALU.mult,
                                                      ALU.mult), drain=True, inc=True)
        return p.op(V, lambda e: e.tensor_scalar(out_ap, v_ap, v3, v1, ALU.mult,
                                                 ALU.add), inc=True, drain=True,
                    waits=(r2,))
    else:
        s3 = p.op(S, lambda e: e.activation(v2, sq, AF.Sqrt), inc=True,
                  waits=(s1, p.now(V)))
        p.op(V, lambda e: e.tensor_scalar(v2, v2, 1.0 / np.sqrt(D), rms_eps, ALU.mult,
                                          ALU.add), drain=True, waits=(s3,))
        r1 = p.op(V, lambda e: e.reciprocal(v3, v2), drain=True, inc=True)
        sa = p.op(V, lambda e: e.tensor_scalar(d1, v_ap, v3, None, ALU.mult), inc=True,
                  drain=True, waits=(r1,))
        return p.op(V, lambda e: e.tensor_tensor(out_ap, d1, scale_bc, ALU.mult),
                    inc=True, waits=(sa,), drain=True)


def transpose_act(c, src_fn, dst):
    p = c.p
    prev = {}
    for tt in range(TT):
        for k in range(KT):
            bnk = (tt * KT + k) % 4
            w = (prev[bnk],) if bnk in prev else ()
            mm = p.op(T, lambda e, tt=tt, k=k, bnk=bnk: e.transpose(
                c.bank(bnk)[:, 0:128], src_fn(tt)[:, k * 128:(k + 1) * 128],
                c.identF[:]), inc=True, waits=w)
            prev[bnk] = p.op(S, lambda e, tt=tt, k=k, bnk=bnk: e.copy(
                out=dst[:, k, tt * 128:(tt + 1) * 128], in_=c.bank(bnk)[:, 0:128]),
                inc=True, waits=(mm,))
    p.barrier((T, S, V))


def softmax_topk(c, psum_ap, ncand, keepk, wout_ap, renorm, bias_bc=None, waits=()):
    p = c.p
    i = c.nstat % 4
    c.nstat += 1
    mx = c.cols[:, i * 8 + 5:i * 8 + 6]
    sm = c.cols[:, i * 8 + 6:i * 8 + 7]
    s2 = c.cols[:, i * 8 + 7:i * 8 + 8]
    lg = c.dump[:, 2, 0:8]
    cnt = c.dump[:, 3, 0:8]
    if bias_bc is not None:
        p.op(V, lambda e: e.tensor_tensor(lg[:, 0:ncand], psum_ap,
                                          bias_bc[:, 0:ncand], ALU.add), drain=True,
             waits=tuple(waits))
    else:
        p.op(V, lambda e: e.tensor_copy(lg[:, 0:ncand], psum_ap), drain=True,
             waits=tuple(waits))
    p.op(V, lambda e: e.tensor_reduce(mx, lg[:, 0:ncand], AX.X, ALU.max, negate=True),
         drain=True, inc=True)
    sx = p.op(S, lambda e: e.activation(lg[:, 0:ncand], lg[:, 0:ncand], AF.Exp,
                                        bias=mx, accum_out=sm), inc=True,
              waits=(p.now(V),))
    p.op(V, lambda e: e.reciprocal(sm, sm), drain=True, waits=(sx,))
    p.op(V, lambda e: e.tensor_scalar(lg[:, 0:ncand], lg[:, 0:ncand], sm, None,
                                      ALU.mult), drain=True)
    p.op(V, lambda e: e.memset(cnt[:, 0:ncand], 0.0), drain=True)
    for j in range(ncand):
        p.op(V, lambda e, j=j: e.scalar_tensor_tensor(
            cnt[:, 0:ncand], lg[:, 0:ncand], lg[:, j:j + 1], cnt[:, 0:ncand],
            ALU.is_lt, ALU.add), drain=True)
    p.op(V, lambda e: e.tensor_scalar(cnt[:, 0:ncand], cnt[:, 0:ncand],
                                      float(keepk) - 0.5, None, ALU.is_lt), drain=True)
    if ncand < 8:
        p.op(V, lambda e: e.memset(wout_ap[:, ncand:8], 0.0), drain=True)
    last = p.op(V, lambda e: e.tensor_tensor(wout_ap[:, 0:ncand], lg[:, 0:ncand],
                                             cnt[:, 0:ncand], ALU.mult), inc=True,
                drain=True)
    if renorm:
        p.op(V, lambda e: e.tensor_reduce(s2, wout_ap[:, 0:ncand], AX.X, ALU.add),
             drain=True)
        p.op(V, lambda e: e.tensor_scalar(s2, s2, 1e-6, None, ALU.add), drain=True)
        p.op(V, lambda e: e.reciprocal(s2, s2), drain=True)
        last = p.op(V, lambda e: e.tensor_scalar(wout_ap[:, 0:ncand],
                                                 wout_ap[:, 0:ncand], s2, None,
                                                 ALU.mult), inc=True, drain=True)
    return last



def step1(c, x0_ap, pf_ap, pr_ap):
    p = c.p
    for tt in range(TT):
        r0 = tt * 128
        lw = (p.now(V), p.now(S))
        l1 = p.dma(Y, "ld1", lambda e, r0=r0: e.dma_start(
            out=c.dump[:, 0, :], in_=pf_ap[r0:r0 + 128, :]), waits=lw)
        l2 = p.dma(Y, "ld1", lambda e, r0=r0: e.dma_start(
            out=c.dump[:, 1, :], in_=pr_ap[r0:r0 + 128, :]))
        l3 = p.dma(Y, "ld1", lambda e, r0=r0: e.dma_start(
            out=c.dump[:, 2, :], in_=x0_ap[r0:r0 + 128, :]))
        v = c.dump[:, 0, :]
        p.op(V, lambda e: e.tensor_tensor(v, v, c.dump[:, 1, :], ALU.add),
             waits=(l3,), drain=True)
        vv = p.op(V, lambda e: e.tensor_tensor(v, v, c.dump[:, 2, :], ALU.add),
                  inc=True, drain=True)
        ln_apply(c, v, c.resid[:, tt, :], kind="ln", vwait=(vv,))
    p.barrier((T, V, S))



def step2(c):
    p = c.p
    transpose_act(c, lambda tt: c.resid[:, tt, :], c.xT)
    for tt in range(TT):
        ts = tt * 128

        def mm_small(e, ts=ts):
            ins = None
            for k in range(KT):
                st, sp = (k == 0), (k == KT - 1)
                e.matmul(c.bank(4)[:, 0:16], c.xT[:, k, ts:ts + 128],
                         c.sharedAT[:, k, :], start=st, stop=sp)
                e.matmul(c.bank(6)[:, 0:96], c.xT[:, k, ts:ts + 128],
                         c.routedAT[:, k, :], start=st, stop=sp)
                ins = e.matmul(c.bank(5)[:, 0:8], c.xT[:, k, ts:ts + 128],
                               c.routerWT[:, k, 0:8], start=st, stop=sp)
            return ins
        mm = p.op(T, mm_small, inc=True, waits=(p.now(S), p.now(V)))
        softmax_topk(c, c.bank(5)[:, 0:E0], E0, 2, c.w8[:, tt, :], True, bias_bc=c.rbc,
                     waits=(mm,))
        hsb = c.gh_f.rearrange("p a b -> p (a b)")[:, 0:112]
        ev1 = p.op(S, lambda e, hsb=hsb: e.copy(out=hsb[:, 0:16], in_=c.bank(4)[:, 0:16]),
                   waits=(mm, p.now(V)))
        ev2 = p.op(S, lambda e, hsb=hsb: e.copy(out=hsb[:, 16:112],
                                                in_=c.bank(6)[:, 0:96]), inc=True)
        for ei in range(E0):
            hv = p.op(V, lambda e, ei=ei, hsb=hsb, tt=tt: e.tensor_scalar(
                hsb[:, 16 + ei * R0:16 + (ei + 1) * R0],
                hsb[:, 16 + ei * R0:16 + (ei + 1) * R0],
                c.w8[:, tt, ei:ei + 1], None, ALU.mult), drain=True, waits=(ev2,),
                inc=(ei == E0 - 1))
        mm2 = p.op(T, lambda e, hsb=hsb: e.transpose(c.bankB(7)[0:16, 0:128],
                                                     hsb[:, 0:16], c.identB[:]),
                   waits=(hv, p.now(S)))
        mm2b = p.op(T, lambda e, hsb=hsb: e.transpose(c.bankB(6)[0:96, 0:128],
                                                      hsb[:, 16:112], c.identB[:]),
                    inc=True)
        p.op(S, lambda e: e.copy(out=c.hTs[:], in_=c.bankB(7)[0:16, 0:128]),
             waits=(mm2b,))
        evT = p.op(S, lambda e: e.copy(out=c.hTr[:], in_=c.bankB(6)[0:96, 0:128]),
                   inc=True)

        def mm_main(e, ts=ts):
            ins = None
            for half in range(2):
                f0 = half * 512
                for k in range(KT):
                    e.matmul(c.bank(half), c.xT[:, k, ts:ts + 128],
                             c.wbufA[:, (half * KT + k) % 2, 0:512],
                             start=(k == 0), stop=False)
                e.matmul(c.bank(half), c.hT[0:16, 0:128], c.sharedBT[0:16, f0:f0 + 512],
                         start=False, stop=False)
                ins = e.matmul(c.bank(half), c.hT[16:112, 0:128],
                               c.routedBcat[:, f0:f0 + 512], start=False, stop=True)
            return ins
        slot_consumer = {0: p.now(T), 1: p.now(T)}
        for half in range(2):
            f0 = half * 512
            for k in range(KT):
                slot = (half * KT + k) % 2
                wld = p.dma(Y, f"ws{slot}", lambda e, k=k, f0=f0, slot=slot: e.dma_start(
                    out=c.wbufA[:, slot, 0:512],
                    in_=c.baseWT[k * 128:(k + 1) * 128, f0:f0 + 512]),
                    waits=(slot_consumer[slot],))
                slot_consumer[slot] = p.op(T, lambda e, k=k, half=half, ts=ts, slot=slot: e.matmul(
                    c.bank(half), c.xT[:, k, ts:ts + 128], c.wbufA[:, slot, 0:512],
                    start=(k == 0), stop=False), inc=True,
                    waits=(wld,) + ((evT,) if (half == 0 and k == 0) else ()))
            p.op(T, lambda e, half=half, f0=f0: e.matmul(
                c.bank(half), c.hTs[:], c.sharedBT[0:16, f0:f0 + 512],
                start=False, stop=False))
            p.op(T, lambda e, half=half, f0=f0: e.matmul(
                c.bank(half), c.hTr[:], c.routedBcat[:, f0:f0 + 512],
                start=False, stop=True), inc=True)
        mm3 = p.now(T)
        v = c.dump[:, 0, :]
        p.op(V, lambda e: e.tensor_tensor(v[:, 0:512], c.bank(0),
                                          c.bcast["base_b"][:, 0:512], ALU.add),
             waits=(mm3,))
        p.op(V, lambda e: e.tensor_tensor(v[:, 512:1024], c.bank(1),
                                          c.bcast["base_b"][:, 512:1024], ALU.add),
             drain=True)
        vv = p.op(V, lambda e, tt=tt: e.tensor_tensor(v, v, c.resid[:, tt, :], ALU.add),
                  inc=True, drain=True)
        ln_apply(c, v, c.resid[:, tt, :], kind="ln", vwait=(vv,))
    p.barrier((T, V, S))



def step3(c, upto3='all'):
    p = c.p
    if upto3 == 'rms1':
        for tt in range(TT):
            ln_apply(c, c.resid[:, tt, :], c.dacc[:, tt, :], kind="rms",
                     scale_bc=c.bcast["scale1"][:])
        p.barrier((T, V, S))
        return
    for tt in range(TT):
        ln_apply(c, c.resid[:, tt, :], c.dump[:, 1, :], kind="rms",
                 scale_bc=c.bcast["scale1"][:])
        for k in range(KT):
            bnk = k % 4
            mm = p.op(T, lambda e, k=k, bnk=bnk: e.transpose(
                c.bank(bnk)[:, 0:128], c.dump[:, 1, k * 128:(k + 1) * 128],
                c.identF[:]), inc=True, waits=(p.now(V), p.now(S)))
            p.op(S, lambda e, k=k, bnk=bnk, tt=tt: e.copy(
                out=c.xT[:, k, tt * 128:(tt + 1) * 128], in_=c.bank(bnk)[:, 0:128]),
                inc=True, waits=(mm,))
        p.barrier((T, V, S))
    for fc in range(4):
        f0 = fc * 512

        def mm_gh(e, f0=f0):
            ins = None
            for k in range(KT):
                ins = e.matmul(c.bank(4, 16), c.wg1T[:, k, :], c.xT[:, k, f0:f0 + 512],
                               start=(k == 0), stop=(k == KT - 1))
            return ins
        mm = p.op(T, mm_gh, inc=True, waits=(p.now(S),))
        p.op(S, lambda e, f0=f0: e.activation(c.ghT[:, f0:f0 + 512], c.bank(4, 16),
                                              AF.Silu, bias=c.bg1c[:]), inc=True,
             waits=(mm,))
    p.barrier((T, V, S))
    if upto3 == 'xt':
        p.op(V, lambda e: e.memset(c.dacc[:], 0.0), inc=True, drain=True)
        for k in range(KT):
            p.op(V, lambda e, k=k: e.tensor_copy(c.dacc[:, k, :],
                                                 c.xT[:, k, 0:1024]), inc=True,
                 drain=True)
        p.barrier((T, V, S))
        return
    p.op(V, lambda e: e.memset(c.dacc[:], 0.0), inc=True, drain=True)
    pairs = [(t, s) for t in range(4) for s in range(t + 1)]
    for sub in range(4):
        tiles = [sub + 4 * b for b in range(4)]
        for hh in range(2):
            h0 = hh * 512
            for bi in range(4):
                ts = tiles[bi] * 128
                mm = p.op(T, lambda e, bi=bi, ts=ts, h0=h0: e.matmul(
                    c.bank(bi), c.ghT[:, ts:ts + 128], c.wg2T[0:16, h0:h0 + 512],
                    start=True, stop=True), inc=True, waits=(p.now(S), p.now(V)))
                zf = c.dump[:, bi % 2, 0:512]
                zv = p.op(V, lambda e, bi=bi, zf=zf, h0=h0: e.tensor_tensor(
                    zf, c.bank(bi), c.bcast["bg2"][:, h0:h0 + 512], ALU.add),
                    inc=True, drain=True, waits=(mm,))
                p.op(S, lambda e, bi=bi, zf=zf: e.activation(
                    c.Pb[:, bi, :], zf, AF.Sigmoid, scale=1.0 / GLA_TEMP), inc=True,
                    waits=(zv,))
                se = p.op(S, lambda e, bi=bi, zf=zf: e.activation(
                    c.dump[:, 2 + bi % 2, 0:512], zf, AF.Exp, scale=-1.0 / GLA_TEMP),
                    inc=True)
                p.op(V, lambda e, bi=bi: e.tensor_scalar(
                    c.Pib[:, bi, :], c.dump[:, 2 + bi % 2, 0:512], 1.0, None, ALU.add),
                    inc=True, drain=True, waits=(se,))
            for bi in range(1, 4):
                p.op(V, lambda e, bi=bi: e.tensor_tensor(
                    c.Pb[:, bi, :], c.Pb[:, bi, :], c.Pb[:, bi - 1, :], ALU.mult),
                    drain=True)
            for bi in range(1, 4):
                p.op(V, lambda e, bi=bi: e.tensor_tensor(
                    c.Pib[:, bi, :], c.Pib[:, bi, :], c.Pib[:, bi - 1, :], ALU.mult),
                    drain=True, inc=(bi == 3))
            if upto3 == 'al1' and sub == DBG_SUB and hh == 0:
                p.barrier((T, V, S))
                return
            slot_consumer = {0: p.now(T), 1: p.now(T)}
            for wd, kind in ((c.wqT, "q"), (c.wkT, "k"), (c.wvT, "v"), (c.wogT, "og")):
                for k in range(KT):
                    slot = k % 2
                    wld = p.dma(Y, f"ws{slot}", lambda e, wd=wd, k=k, h0=h0,
                                slot=slot: e.dma_start(
                        out=c.wbufA[:, slot, 0:512],
                        in_=wd[k * 128:(k + 1) * 128, h0:h0 + 512]),
                        waits=(slot_consumer[slot],))

                    def mm_w(e, k=k, tiles=tiles):
                        ins = None
                        for bi in range(4):
                            ts = tiles[bi] * 128
                            ins = e.matmul(c.bank(4 + bi), c.xT[:, k, ts:ts + 128],
                                           c.wbufA[:, k % 2, 0:512], start=(k == 0),
                                           stop=(k == KT - 1))
                        return ins
                    last_mm = p.op(T, mm_w, inc=True,
                                   waits=(wld,) + ((p.now(S), p.now(V)) if k == 0 else ()))
                    slot_consumer[slot] = last_mm
                sil = {}
                for bi in range(4):
                    if kind == "q":
                        p.op(S, lambda e, bi=bi: e.copy(out=c.qb[:, bi, :],
                                                        in_=c.bank(4 + bi)), inc=True,
                             waits=(last_mm,))
                    elif kind == "k":
                        p.op(S, lambda e, bi=bi: e.copy(out=c.kb[:, bi, :],
                                                        in_=c.bank(4 + bi)), inc=True,
                             waits=(last_mm,))
                    elif kind == "v":
                        p.op(S, lambda e, bi=bi: e.copy(out=c.vb[:, bi, :],
                                                        in_=c.bank(4 + bi)), inc=True,
                             waits=(last_mm,))
                    else:
                        ov = p.op(V, lambda e, bi=bi, h0=h0: e.tensor_tensor(
                            c.dump[:, bi % 2, 0:512], c.bank(4 + bi),
                            c.bcast["bog"][:, h0:h0 + 512], ALU.add), inc=True,
                            drain=True,
                            waits=(last_mm,) + ((sil[bi - 2],) if bi - 2 in sil else ()))
                        sil[bi] = p.op(S, lambda e, bi=bi, h0=h0: e.activation(
                            c.og[:, bi, h0:h0 + 512], c.dump[:, bi % 2, 0:512],
                            AF.Silu), inc=True, waits=(ov,))
            p.barrier((T, V, S))
            if upto3 in ('q', 'k', 'v', 'og', 'alpha') and sub == DBG_SUB and hh == 0:
                src = {'q': c.qb, 'k': c.kb, 'v': c.vb, 'og': c.og, 'alpha': c.Pb}[upto3]
                wid = 1024 if upto3 == 'og' else 512
                for bi in range(4):
                    p.op(V, lambda e, bi=bi, src=src, wid=wid: e.tensor_copy(
                        c.dacc[:, bi, 0:wid], src[:, bi, 0:wid]), inc=True, drain=True)
                p.barrier((T, V, S))
                return
            for bi in range(4):
                pi = pairs.index((bi, bi))
                tm = c.dump[:, 2 + bi % 2, 0:512] if bi < 2 else c.dump[:, 2 + bi % 2, 512:1024]
                p.op(V, lambda e, bi=bi, tm=tm: e.tensor_tensor(
                    tm, c.qb[:, bi, :], c.kb[:, bi, :], ALU.mult), drain=True)
                p.op(V, lambda e, pi=pi, tm=tm: e.tensor_reduce(
                    c.sc[:, pi, :], tm.rearrange("p (h k) -> p h k", k=DK), AX.X,
                    ALU.add), drain=True, inc=True)
            for bi in range(4):
                p.op(V, lambda e, bi=bi: e.tensor_tensor(
                    c.qb[:, bi, :], c.qb[:, bi, :], c.Pb[:, bi, :], ALU.mult))
            for bi in range(4):
                p.op(V, lambda e, bi=bi: e.tensor_tensor(
                    c.kb[:, bi, :], c.kb[:, bi, :], c.Pib[:, bi, :], ALU.mult),
                    inc=(bi == 3))
            for pi, (t_, s_) in enumerate(pairs):
                if t_ == s_:
                    continue
                tm = c.dump[:, 2 + pi % 2, 0:512] if pi % 4 < 2 else c.dump[:, 2 + pi % 2, 512:1024]
                p.op(V, lambda e, t_=t_, s_=s_, tm=tm: e.tensor_tensor(
                    tm, c.qb[:, t_, :], c.kb[:, s_, :], ALU.mult), drain=True)
                p.op(V, lambda e, pi=pi, tm=tm: e.tensor_reduce(
                    c.sc[:, pi, :], tm.rearrange("p (h k) -> p h k", k=DK), AX.X,
                    ALU.add), drain=True, inc=True)
            p.op(V, lambda e: e.memset(c.oac[:], 0.0), drain=True)
            for pi, (t_, s_) in enumerate(pairs):
                tm = c.dump[:, 2 + pi % 2, 0:512] if pi % 4 < 2 else c.dump[:, 2 + pi % 2, 512:1024]
                scbc = c.sc[:, pi, :].unsqueeze(2).to_broadcast([128, 8, DK])
                p.op(V, lambda e, s_=s_, tm=tm, scbc=scbc: e.tensor_tensor(
                    tm.rearrange("p (h k) -> p h k", k=DK), scbc,
                    c.vb[:, s_, :].rearrange("p (h k) -> p h k", k=DK), ALU.mult),
                    drain=True)
                p.op(V, lambda e, t_=t_, tm=tm: e.tensor_tensor(
                    c.oac[:, t_, :], c.oac[:, t_, :], tm, ALU.add), drain=True)
            if upto3 == 'o' and sub == DBG_SUB and hh == 0:
                for bi in range(4):
                    p.op(V, lambda e, bi=bi: e.tensor_copy(
                        c.dacc[:, bi, 0:512], c.oac[:, bi, :]), inc=True, drain=True)
                p.barrier((T, V, S))
                return
            for bi in range(4):
                i8 = c.sc[:, 10, :]
                r8 = c.sc[:, 11, :]
                t8 = c.cols[:, 40:48]
                ov = c.oac[:, bi, :]
                o3 = ov.rearrange("p (h k) -> p h k", k=DK)
                p.op(V, lambda e, o3=o3, i8=i8: e.tensor_reduce(i8, o3, AX.X, ALU.add),
                     drain=True, inc=True)
                sq = c.dump[:, 2, 0:512]
                sqo = p.op(S, lambda e, ov=ov, sq=sq: e.activation(sq, ov, AF.Square),
                           inc=True, waits=(p.now(V),))
                p.op(V, lambda e, sq=sq, r8=r8: e.tensor_reduce(
                    r8, sq.rearrange("p (h k) -> p h k", k=DK), AX.X, ALU.add),
                    drain=True, waits=(sqo,))
                p.op(V, lambda e, i8=i8: e.tensor_scalar(i8, i8, 1.0 / DK, None,
                                                         ALU.mult), drain=True)
                p.op(V, lambda e, i8=i8, t8=t8: e.tensor_tensor(t8, i8, i8, ALU.mult),
                     drain=True)
                p.op(V, lambda e, r8=r8, t8=t8: e.scalar_tensor_tensor(
                    r8, r8, 1.0 / DK, t8, ALU.mult, ALU.subtract), drain=True)
                sr = p.op(S, lambda e, r8=r8: e.activation(r8, r8, AF.Sqrt, bias=c.epsc[:]),
                          inc=True, waits=(p.now(V),))
                p.op(V, lambda e, r8=r8: e.reciprocal(r8, r8), drain=True, waits=(sr,))
                p.op(V, lambda e, o3=o3, i8=i8: e.tensor_tensor(
                    o3, o3, i8.unsqueeze(2).to_broadcast([128, 8, DK]), ALU.subtract),
                    drain=True)
                p.op(V, lambda e, o3=o3, r8=r8: e.tensor_tensor(
                    o3, o3, r8.unsqueeze(2).to_broadcast([128, 8, DK]), ALU.mult),
                    drain=True)
                p.op(V, lambda e, bi=bi, ov=ov, h0=h0: e.tensor_tensor(
                    c.og[:, bi, h0:h0 + 512], ov, c.og[:, bi, h0:h0 + 512], ALU.mult),
                    inc=True, drain=True)
            p.barrier((V, S, T))
            if upto3 == 'oln' and sub == DBG_SUB and hh == 0:
                for bi in range(4):
                    p.op(V, lambda e, bi=bi: e.tensor_copy(
                        c.dacc[:, bi, 0:512], c.oac[:, bi, :]), inc=True, drain=True)
                p.barrier((T, V, S))
                return
        if upto3 == 'og2' and sub == DBG_SUB:
            for bi in range(4):
                p.op(V, lambda e, bi=bi: e.tensor_copy(
                    c.dacc[:, bi, :], c.og[:, bi, :]), inc=True, drain=True)
            p.barrier((T, V, S))
            return
        if upto3 == 'hh1' and sub == DBG_SUB:
            p.barrier((T, V, S))
            return
        for bi in range(4):
            for k in range(KT):
                bnk = k % 4
                mm = p.op(T, lambda e, bi=bi, k=k, bnk=bnk: e.transpose(
                    c.bankB(bnk)[:, 0:128], c.og[:, bi, k * 128:(k + 1) * 128],
                    c.identB[:]), inc=True, waits=(p.now(S), p.now(V)))
                p.op(S, lambda e, bi=bi, k=k, bnk=bnk: e.copy(
                    out=c.ogT[:, k, bi * 128:(bi + 1) * 128],
                    in_=c.bankB(bnk)[:, 0:128]), inc=True, waits=(mm,))
        p.barrier((T, S, V))
        slot_consumer = {0: p.now(T), 1: p.now(T)}
        for bi in range(4):
            tt = tiles[bi]
            for k in range(KT):
                slot = k % 2
                wld = p.dma(Y, f"ws{slot}", lambda e, k=k, slot=slot: e.dma_start(
                    out=c.wbufA[:, slot, :], in_=c.woT[k * 128:(k + 1) * 128, :]),
                    waits=(slot_consumer[slot],))

                def mm_wo(e, bi=bi, k=k, slot=slot):
                    e.matmul(c.bank(4), c.ogT[:, k, bi * 128:(bi + 1) * 128],
                             c.wbufA[:, slot, 0:512], start=(k == 0),
                             stop=(k == KT - 1))
                    return e.matmul(c.bank(5), c.ogT[:, k, bi * 128:(bi + 1) * 128],
                                    c.wbufA[:, slot, 512:1024], start=(k == 0),
                                    stop=(k == KT - 1))
                slot_consumer[slot] = p.op(T, mm_wo, inc=True,
                     waits=(wld,) + ((p.now(V), p.now(S)) if k == 0 else ()))
            ev = p.op(V, lambda e, tt=tt: e.tensor_copy(c.dacc[:, tt, 0:512],
                                                        c.bank(4)), drain=True,
                      waits=(p.now(T),))
            p.op(V, lambda e, tt=tt: e.tensor_copy(c.dacc[:, tt, 512:1024],
                                                   c.bank(5)), inc=True, drain=True)
        if upto3 == 'wo1' and sub == DBG_SUB:
            p.barrier((T, V, S))
            return
    p.barrier((T, V, S))
    if upto3 == 'gla':
        return
    for tt in range(TT):
        v = c.dump[:, 1, :]
        vv = p.op(V, lambda e, tt=tt: e.tensor_tensor(v, c.resid[:, tt, :],
                                                      c.dacc[:, tt, :], ALU.add),
                  inc=True, drain=True, waits=(p.now(S),))
        ln_apply(c, v, c.dump[:, 0, :], kind="rms", scale_bc=c.bcast["scale2"][:],
                 vwait=(vv,))
        for k in range(KT):
            bnk = k % 4
            mm = p.op(T, lambda e, k=k, bnk=bnk: e.transpose(
                c.bank(bnk)[:, 0:128], c.dump[:, 0, k * 128:(k + 1) * 128],
                c.identF[:]), inc=True, waits=(p.now(V), p.now(S)))
            p.op(S, lambda e, k=k, bnk=bnk, tt=tt: e.copy(
                out=c.xT[:, k, tt * 128:(tt + 1) * 128], in_=c.bank(bnk)[:, 0:128]),
                inc=True, waits=(mm,))
        p.barrier((T, V, S))
    if upto3 == 'rms2':
        return
    ffn_slab(c, [("g", c.wgateT, c.wupT, c.wdownT, None, DHID // 512)], c.dacc, step=3)
    p.barrier((T, V, S))
    for tt in range(TT):
        v = c.dump[:, 1, :]
        vv = p.op(V, lambda e, tt=tt: e.scalar_tensor_tensor(
            v, c.resid[:, tt, :], 2.0, c.dacc[:, tt, :], ALU.mult, ALU.add),
            inc=True, drain=True, waits=(p.now(S),))
        ln_apply(c, v, c.resid[:, tt, :], kind="ln", vwait=(vv,))
    p.barrier((T, V, S))



def step4(c, out_ap):
    p = c.p
    transpose_act(c, lambda tt: c.resid[:, tt, :], c.xT)
    for tt in range(TT):
        ts = tt * 128

        def mm_r(e, ts=ts):
            ins = None
            for k in range(KT):
                ins = e.matmul(c.bank(6)[:, 0:8], c.xT[:, k, ts:ts + 128],
                               c.router1[:, k, :], start=(k == 0), stop=(k == KT - 1))
            return ins
        mm = p.op(T, mm_r, inc=True, waits=(p.now(S), p.now(V)))
        p.wait(V, mm)
        softmax_topk(c, c.bank(6)[:, 0:E1], E1, 6, c.w8[:, tt, :], False)
    p.barrier((T, V, S))
    p.op(V, lambda e: e.memset(c.dacc[:], 0.0), inc=True, drain=True)
    specs = []
    for ei in range(E1):
        specs.append(("exp", ei, c.f1gate[ei], c.f1up[ei], c.f1down[ei]))
    specs.append(("shr", 0, c.wsgate[:, 0:512], c.wsup[:, 0:512], c.wsdown[0:512, :]))
    specs.append(("shr", 1, c.wsgate[:, 512:1024], c.wsup[:, 512:1024],
                  c.wsdown[512:1024, :]))
    ffn_slab(c, [(kind, wg, wu, wd, ei, 1) for kind, ei, wg, wu, wd in specs],
             c.dacc, step=4)
    p.barrier((T, V, S))
    for tt in range(TT):
        v = c.dump[:, 1, :]
        vv = p.op(V, lambda e, tt=tt: e.tensor_tensor(v, c.resid[:, tt, :],
                                                      c.dacc[:, tt, :], ALU.add),
                  inc=True, drain=True, waits=(p.now(S),))
        lo = ln_apply(c, v, c.dump[:, 0, :], kind="ln", vwait=(vv,))
        p.dma(G, "st", lambda e, tt=tt: e.dma_start(
            out=out_ap[tt * 128:(tt + 1) * 128, :], in_=c.dump[:, 0, :]),
            waits=(lo,))
        p.wait(V, p.dnow("st"))
        p.wait(S, p.dnow("st"))
    p.wait(G, p.dnow("st"))
    p.op(G, lambda e: e.nop())


def ffn_slab(c, units, dacc, step):
    p = c.p
    slab_last_T = p.now(T)
    for kind, wg, wu, wd, ei, nfc in units:
        for fc in range(nfc):
            f0 = fc * 512
            lds = []
            for k in range(KT):
                if step == 3:
                    ga = wg[k * 128:(k + 1) * 128, f0:f0 + 512]
                    ua = wu[k * 128:(k + 1) * 128, f0:f0 + 512]
                else:
                    ga = wg[k * 128:(k + 1) * 128, 0:512]
                    ua = wu[k * 128:(k + 1) * 128, 0:512]
                lds.append(p.dma(Y, "wsl", lambda e, k=k, ga=ga: e.dma_start(
                    out=c.slab_g[:, k, :], in_=ga), waits=(slab_last_T,)))
                lds.append(p.dma(G, "wsl2", lambda e, k=k, ua=ua: e.dma_start(
                    out=c.slab_u[:, k, :], in_=ua), waits=(slab_last_T,)))
            for k4 in range(4):
                if step == 3:
                    da = wd[f0 + k4 * 128:f0 + (k4 + 1) * 128, :]
                else:
                    da = wd[k4 * 128:(k4 + 1) * 128, :]
                lds.append(p.dma(Y, "wsl", lambda e, k4=k4, da=da: e.dma_start(
                    out=c.slab_d[:, k4, :], in_=da), waits=(slab_last_T,)))
            slab_ok = (p.dnow("wsl"), p.dnow("wsl2"))
            for tt in range(TT):
                ts = tt * 128

                def mm_gu(e, ts=ts):
                    ins = None
                    for k in range(KT):
                        e.matmul(c.bank(0), c.xT[:, k, ts:ts + 128],
                                 c.slab_g[:, k, :], start=(k == 0), stop=(k == KT - 1))
                        ins = e.matmul(c.bank(1), c.xT[:, k, ts:ts + 128],
                                       c.slab_u[:, k, :], start=(k == 0),
                                       stop=(k == KT - 1))
                    return ins
                gu = p.op(T, mm_gu, inc=True,
                          waits=slab_ok + (p.now(S), p.now(V)))
                sg = p.op(S, lambda e: e.activation(c.gh_f[:, 0, :], c.bank(0),
                                                    AF.Silu), inc=True, waits=(gu,))
                su = p.op(S, lambda e: e.copy(out=c.uh_f[:, 0, :], in_=c.bank(1)),
                          inc=True)
                hv = p.op(V, lambda e: e.tensor_tensor(
                    c.gh_f[:, 1, :], c.gh_f[:, 0, :], c.uh_f[:, 0, :], ALU.mult),
                    inc=True, drain=True, waits=(su,))
                if kind == "exp":
                    hv = p.op(V, lambda e, ei=ei, tt=tt: e.tensor_scalar(
                        c.gh_f[:, 1, :], c.gh_f[:, 1, :], c.w8[:, tt, ei:ei + 1],
                        None, ALU.mult), inc=True, drain=True)
                for k4 in range(4):
                    bnk = 2 + k4 % 2
                    mm = p.op(T, lambda e, k4=k4, bnk=bnk: e.transpose(
                        c.bankB(bnk)[:, 0:128], c.gh_f[:, 1, k4 * 128:(k4 + 1) * 128],
                        c.identB[:]), inc=True, waits=(hv, p.now(S)))
                    p.op(S, lambda e, k4=k4, bnk=bnk: e.copy(
                        out=c.hT[:, k4 * 128:(k4 + 1) * 128],
                        in_=c.bankB(bnk)[:, 0:128]), inc=True, waits=(mm,))
                hTready = p.now(S)

                def mm_dn(e):
                    ins = None
                    for k4 in range(4):
                        e.matmul(c.bank(4), c.hT[:, k4 * 128:(k4 + 1) * 128],
                                 c.slab_d[:, k4, 0:512], start=(k4 == 0),
                                 stop=(k4 == 3))
                        ins = e.matmul(c.bank(5), c.hT[:, k4 * 128:(k4 + 1) * 128],
                                       c.slab_d[:, k4, 512:1024], start=(k4 == 0),
                                       stop=(k4 == 3))
                    return ins
                dn = p.op(T, mm_dn, inc=True, waits=(hTready, p.now(V)))
                slab_last_T = dn
                p.op(V, lambda e, tt=tt: e.tensor_tensor(
                    dacc[:, tt, 0:512], dacc[:, tt, 0:512], c.bank(4), ALU.add),
                    drain=True, waits=(dn,))
                p.op(V, lambda e, tt=tt: e.tensor_tensor(
                    dacc[:, tt, 512:1024], dacc[:, tt, 512:1024], c.bank(5), ALU.add),
                    inc=True, drain=True)



_MCFG = dict(L=4096, DM=1024, DI=2048, NS=16, DTR=64, Tc=512)

WNAMES = [
    "L0_ffn_base_W", "L0_ffn_base_b", "L0_ffn_shared_A", "L0_ffn_shared_B",
    "L0_ffn_routed_A", "L0_ffn_routed_B", "L0_ffn_router_W", "L0_ffn_router_b",
    "L1_attn_scale1", "L1_attn_scale2", "L1_attn_wq", "L1_attn_wk", "L1_attn_wv",
    "L1_attn_wg1", "L1_attn_bg1", "L1_attn_wg2", "L1_attn_bg2", "L1_attn_wog",
    "L1_attn_bog", "L1_attn_wo", "L1_attn_w_gate", "L1_attn_w_up", "L1_attn_w_down",
    "L1_ffn_router_w", "L1_ffn_w_gate", "L1_ffn_w_up", "L1_ffn_w_down",
    "L1_ffn_ws_gate", "L1_ffn_ws_up", "L1_ffn_ws_down",
]

_WSHAPES = {
    "L0_ffn_base_W": (1024, 1024), "L0_ffn_base_b": (1024,),
    "L0_ffn_shared_A": (16, 1024), "L0_ffn_shared_B": (1024, 16),
    "L0_ffn_routed_A": (6, 16, 1024), "L0_ffn_routed_B": (6, 1024, 16),
    "L0_ffn_router_W": (6, 1024), "L0_ffn_router_b": (6,),
    "L1_attn_scale1": (1024,), "L1_attn_scale2": (1024,),
    "L1_attn_wq": (1024, 1024), "L1_attn_wk": (1024, 1024),
    "L1_attn_wv": (1024, 1024), "L1_attn_wg1": (16, 1024),
    "L1_attn_bg1": (16,), "L1_attn_wg2": (1024, 16), "L1_attn_bg2": (1024,),
    "L1_attn_wog": (1024, 1024), "L1_attn_bog": (1024,),
    "L1_attn_wo": (1024, 1024), "L1_attn_w_gate": (4096, 1024),
    "L1_attn_w_up": (4096, 1024), "L1_attn_w_down": (1024, 4096),
    "L1_ffn_router_w": (1024, 8), "L1_ffn_w_gate": (8, 1024, 512),
    "L1_ffn_w_up": (8, 1024, 512), "L1_ffn_w_down": (8, 512, 1024),
    "L1_ffn_ws_gate": (1024, 1024), "L1_ffn_ws_up": (1024, 1024),
    "L1_ffn_ws_down": (1024, 1024),
}


def _mamba_host_layout(w, cfg):
    DTt = cfg["DI"] // 128
    out = {}
    out["x0"] = np.ascontiguousarray(w["x0"])
    for nm in ("in_proj", "out_proj", "x_proj", "dt_w"):
        out[nm] = np.ascontiguousarray(w[nm])
    out["A_log"] = np.ascontiguousarray(
        w["A_log"].reshape(DTt, 128, cfg["NS"]).transpose(1, 0, 2))
    out["dt_b"] = np.ascontiguousarray(w["dt_b"].reshape(DTt, 128).T)
    out["conv_b"] = np.ascontiguousarray(w["conv_b"].reshape(DTt, 128).T)
    out["D"] = np.ascontiguousarray(w["D"].reshape(DTt, 128).T)
    out["conv_w"] = np.ascontiguousarray(
        w["conv_w"].reshape(cfg["DI"], 4).reshape(DTt, 128, 4).transpose(1, 0, 2))
    sel = np.zeros((cfg["NS"], cfg["NS"] * 128), BF)
    for n in range(cfg["NS"]):
        sel[n, n * 128:(n + 1) * 128] = 1
    out["sel16"] = sel
    return out


def _build_launch1(cfg):
    nc = bass.Bass("TRN2", target_bir_lowering=False)
    L, DM, DI, NS_, DTR = cfg["L"], cfg["DM"], cfg["DI"], cfg["NS"], cfg["DTR"]
    DTt = DI // 128
    tens = {
        "x0": nc.dram_tensor("x0", [L, DM], F32, kind="ExternalInput").ap(),
        "in_proj": nc.dram_tensor("in_proj", [2 * DI, DM], F32, kind="ExternalInput").ap(),
        "out_proj": nc.dram_tensor("out_proj", [DM, DI], F32, kind="ExternalInput").ap(),
        "x_proj": nc.dram_tensor("x_proj", [DTR + 2 * NS_, DI], F32, kind="ExternalInput").ap(),
        "dt_w": nc.dram_tensor("dt_w", [DI, DTR], F32, kind="ExternalInput").ap(),
        "A_log": nc.dram_tensor("A_log", [128, DTt, NS_], F32, kind="ExternalInput").ap(),
        "dt_b": nc.dram_tensor("dt_b", [128, DTt], F32, kind="ExternalInput").ap(),
        "conv_b": nc.dram_tensor("conv_b", [128, DTt], F32, kind="ExternalInput").ap(),
        "D": nc.dram_tensor("D", [128, DTt], F32, kind="ExternalInput").ap(),
        "conv_w": nc.dram_tensor("conv_w", [128, DTt, 4], F32, kind="ExternalInput").ap(),
        "sel16": nc.dram_tensor("sel16", [NS_, NS_ * 128], BF16, kind="ExternalInput").ap(),
        "out": nc.dram_tensor("out", [L, DM], F32, kind="ExternalOutput").ap(),
    }
    p = Prog()
    stack = ExitStack()
    with stack:
        build_mamba_fwd(nc, p, stack, cfg, tens)
        p.build(nc, stack)
    return nc


def _build_launch2():
    nc = bass.Bass("TRN2", target_bir_lowering=False)
    tens = {}
    for nm in WNAMES:
        tens[nm] = nc.dram_tensor(nm, list(_WSHAPES[nm]), F32, kind="ExternalInput").ap()
    tens["x0s"] = nc.dram_tensor("x0s", [2048, 1024], F32, kind="ExternalInput").ap()
    tens["pfs"] = nc.dram_tensor("pfs", [2048, 1024], F32, kind="ExternalInput").ap()
    tens["prs"] = nc.dram_tensor("prs", [2048, 1024], F32, kind="ExternalInput").ap()
    tens["identF"] = nc.dram_tensor("identF", [128, 128], F32, kind="ExternalInput").ap()
    tens["identB"] = nc.dram_tensor("identB", [128, 128], BF16, kind="ExternalInput").ap()
    out = nc.dram_tensor("out", [2048, 1024], F32, kind="ExternalOutput").ap()
    p = Prog(inc_all=True)
    stack = ExitStack()
    with stack:
        c = alloc_common(nc, p, stack, tens)
        step1(c, tens["x0s"], tens["pfs"], tens["prs"])
        step2(c)
        step3(c)
        step4(c, out)
        p.build(nc, stack)
    return nc


def kernel(**inputs):
    inputs = {k: np.asarray(v, np.float32) for k, v in inputs.items()}
    x_full = inputs["L0_attn_hidden_states"]

    nc1 = _build_launch1(_MCFG)
    in_maps1 = []
    for cidx in range(8):
        b, d = cidx // 2, cidx % 2
        sfx = "f" if d == 0 else "r"
        w = dict(
            x0=x_full[b] if d == 0 else np.ascontiguousarray(x_full[b][::-1]),
            in_proj=inputs[f"L0_attn_in_proj_{sfx}"],
            conv_w=inputs[f"L0_attn_conv_w_{sfx}"],
            conv_b=inputs[f"L0_attn_conv_b_{sfx}"],
            x_proj=inputs[f"L0_attn_x_proj_{sfx}"],
            dt_w=inputs[f"L0_attn_dt_w_{sfx}"],
            dt_b=inputs[f"L0_attn_dt_b_{sfx}"],
            A_log=inputs[f"L0_attn_A_log_{sfx}"],
            D=inputs[f"L0_attn_D_{sfx}"],
            out_proj=inputs[f"L0_attn_out_proj_{sfx}"],
        )
        in_maps1.append(_mamba_host_layout(w, _MCFG))
    res1 = run_bass_kernel_spmd(nc1, in_maps1, core_ids=list(range(8)))
    pf = [np.asarray(res1.results[2 * b]["out"]) for b in range(4)]
    pr = [np.ascontiguousarray(np.asarray(res1.results[2 * b + 1]["out"])[::-1])
          for b in range(4)]

    nc2 = _build_launch2()
    identF = np.eye(128, dtype=np.float32)
    identB = np.eye(128, dtype=BF)
    in_maps2 = []
    for cidx in range(8):
        sl = slice(512 * cidx, 512 * (cidx + 1))
        im = {nm: inputs[nm] for nm in WNAMES}
        im["x0s"] = np.ascontiguousarray(
            np.concatenate([x_full[b, sl] for b in range(4)], 0))
        im["pfs"] = np.ascontiguousarray(
            np.concatenate([pf[b][sl] for b in range(4)], 0))
        im["prs"] = np.ascontiguousarray(
            np.concatenate([pr[b][sl] for b in range(4)], 0))
        im["identF"] = identF
        im["identB"] = identB
        in_maps2.append(im)
    res2 = run_bass_kernel_spmd(nc2, in_maps2, core_ids=list(range(8)))

    out = np.zeros((4, 4096, 1024), np.float32)
    for cidx in range(8):
        o = np.asarray(res2.results[cidx]["out"]).reshape(4, 512, 1024)
        out[:, 512 * cidx:512 * (cidx + 1), :] = o
    return out
